# version 11
# baseline (speedup 1.0000x reference)
import numpy as np
import ml_dtypes
from contextlib import ExitStack

import concourse.bass as bass
import concourse.mybir as mybir
from concourse.bass_utils import run_bass_kernel_spmd

F32 = mybir.dt.float32
BF16 = mybir.dt.bfloat16
I32 = mybir.dt.int32
AF = mybir.ActivationFunctionType
ALU = mybir.AluOpType
AX = mybir.AxisListType

NCORE = 8
SEQ = 16384
E = 1024
NH = 16
DH = 64
HID = 3072
TOK = 2048
A = TOK + 6
QS = 122
NSUB = 17
SUBQ = [122] * 16 + [96]
SUBK = [128] * 16 + [102]
QUADS = [(0, 4), (4, 8), (8, 12), (12, 16), (16, 17)]
R5 = [(0, 512), (512, 1024), (1024, 1536), (1536, 2048), (2048, 2054)]
INV127 = 1.0 / 127.0
INV127SQ = INV127 * INV127

STAGE = 4
DEBUG = False
DBGQ = 0

ENGS = ("T", "V", "S", "Y", "G")
BLOCK_ATTR = {"T": "tensor", "V": "vector", "S": "scalar", "Y": "sync", "G": "gpsimd"}
DMA_ENGS = ("Y", "G")


class _Op:
    __slots__ = ("eng", "name", "args", "kw", "deps", "gdeps", "inc", "cum", "idx", "fragile",
                 "grp")


class Sched:
    def __init__(self):
        self.ops = {e: [] for e in ENGS}
        self.res = {}
        self.gcnt = {}
        self.glast = {}

    def new_res(self, key, fence=None):
        self.res[key] = {"w": None, "r": [], "f": list(fence or [])}

    def fence(self):
        return [(e, len(self.ops[e]) - 1) for e in ENGS if self.ops[e]]

    def op(self, eng, name, args=(), kw=None, reads=(), writes=(), extra_deps=(), latch=False,
           fragile=False, grp=None):
        idx = len(self.ops[eng])
        deps = {}
        gdeps = {}

        def add(d):
            de, di = d
            if di < 0 or (de == eng and di >= idx):
                return
            t = self.ops[de][di]
            if de in DMA_ENGS and t.name == "dma_start":
                g = t.grp
                c = self.gcnt[g]
                if gdeps.get(g, 0) < c:
                    gdeps[g] = c
                return
            if de in DMA_ENGS:
                return
            if de == eng and not latch and not (idx - di <= 2):
                return
            if deps.get(de, -1) < di:
                deps[de] = di

        for k in reads:
            r = self.res[k]
            for dd in r["f"]:
                add(dd)
            r["f"] = []
            if r["w"]:
                add(r["w"])
            r["r"].append((eng, idx))
        for k in writes:
            r = self.res[k]
            for dd in r["f"]:
                add(dd)
            r["f"] = []
            if r["w"]:
                add(r["w"])
            for dd in r["r"]:
                add(dd)
            r["r"] = []
            r["w"] = (eng, idx)
        for dd in extra_deps:
            add(dd)
        o = _Op()
        o.eng, o.name, o.args, o.kw = eng, name, args, (kw or {})
        o.deps, o.gdeps, o.inc, o.idx = deps, gdeps, False, idx
        o.fragile = fragile
        o.grp = None
        if eng in DMA_ENGS and name == "dma_start":
            assert grp is not None, "dma_start needs a semaphore group"
            o.grp = grp
            self.gcnt[grp] = self.gcnt.get(grp, 0) + 1
            self.glast[grp] = (eng, idx)
        for de, di in deps.items():
            self.ops[de][di].inc = True
        self.ops[eng].append(o)
        return (eng, idx)

    def finalize(self):
        for e in ENGS:
            c = 0
            for o in self.ops[e]:
                if o.grp is None and o.inc:
                    c += 1
                o.cum = c

    def emit(self, block, sems):
        self.finalize()

        def runner(e):
            def run(engobj):
                mw = {}
                for o in self.ops[e]:
                    for de in sorted(o.deps):
                        di = o.deps[de]
                        val = self.ops[de][di].cum
                        assert val > 0, f"{e} op {o.name} depends on un-inc'd {de}[{di}]"
                        if mw.get(de, -1) >= val:
                            continue
                        mw[de] = val
                        engobj.wait_ge(sems[de], val)
                    for g in sorted(o.gdeps):
                        val = 16 * o.gdeps[g]
                        if mw.get(g, -1) >= val:
                            continue
                        mw[g] = val
                        engobj.wait_ge(sems[g], val)
                    if o.name == "wait_only":
                        continue
                    inst = getattr(engobj, o.name)(*o.args, **o.kw)
                    if o.grp is not None:
                        inst.then_inc(sems[o.grp], 16)
                    elif o.inc:
                        inst.then_inc(sems[e], 1)
            return run

        for e in ENGS:
            getattr(block, BLOCK_ATTR[e])(runner(e))


def _ln(S, pb, PB, c_ones, c_onesrow, stt, stt_key, src_sb, src_key, zb, zb_key,
        zsq, zsq_keys, dst_sb, dst_key, n, cols, banks, dst_cols=None):
    if dst_cols is None:
        dst_cols = cols
    """LayerNorm over the feature dim (8 chunks x 128 partitions) for n tokens.

    zb must hold a bf16 copy of src (stats operand). Overwrites src in place
    (src - mean); writes the normalized fp32 result into dst.
    """
    sb1, sb2, bb1, bb2 = banks
    m_sb = stt[0:1, 0:n]
    e2_sb = stt[0:1, 512:512 + n]
    m2_sb = stt[0:1, 1024:1024 + n]
    var_sb = stt[0:1, 1536:1536 + n]
    lnv_sb = stt[0:1, 1024:1024 + n]
    rstd_sb = stt[0:1, 2048:2048 + n]
    for c in range(8):
        S.op("T", "matmul", (pb[sb1][0:1, 0:n], c_ones[:, 0:1], zb[0][:, zb[1][c]:zb[1][c] + n]),
             dict(start=(c == 0), stop=(c == 7), skip_group_check=True),
             reads=("consts", zb_key), writes=(PB[sb1],))
    for c in range(8):
        zs = zsq[c % 2]
        S.op("S", "activation", (zs[:, 0:n], zb[0][:, zb[1][c]:zb[1][c] + n], AF.Square),
             reads=(zb_key,), writes=(zsq_keys[c % 2],))
        S.op("T", "matmul", (pb[sb2][0:1, 0:n], c_ones[:, 0:1], zs[:, 0:n]),
             dict(start=(c == 0), stop=(c == 7), skip_group_check=True),
             reads=("consts", zsq_keys[c % 2]), writes=(PB[sb2],))
    S.op("V", "tensor_scalar", (m_sb, pb[sb1][0:1, 0:n], 1.0 / E, None, ALU.mult),
         reads=(PB[sb1],), writes=(stt_key,))
    S.op("V", "tensor_scalar", (e2_sb, pb[sb2][0:1, 0:n], 1.0 / E, None, ALU.mult),
         reads=(PB[sb2],), writes=(stt_key,))
    S.op("V", "tensor_tensor", (m2_sb, m_sb, m_sb, ALU.mult), reads=(stt_key,), writes=(stt_key,))
    S.op("V", "tensor_tensor", (var_sb, e2_sb, m2_sb, ALU.subtract), reads=(stt_key,), writes=(stt_key,))
    S.op("V", "tensor_scalar", (var_sb, var_sb, 1e-5, None, ALU.add),
         reads=(stt_key,), writes=(stt_key,))
    S.op("S", "activation", (lnv_sb, var_sb, AF.Ln),
         reads=(stt_key,), writes=(stt_key,))
    S.op("S", "activation", (rstd_sb, lnv_sb, AF.Exp), dict(scale=-0.5),
         reads=(stt_key,), writes=(stt_key,))
    S.op("T", "matmul", (pb[bb1][:, 0:n], c_onesrow[0:1, :], m_sb),
         dict(start=True, stop=True, skip_group_check=True),
         reads=("consts", stt_key), writes=(PB[bb1],))
    S.op("T", "matmul", (pb[bb2][:, 0:n], c_onesrow[0:1, :], rstd_sb),
         dict(start=True, stop=True, skip_group_check=True),
         reads=("consts", stt_key), writes=(PB[bb2],))
    S.op("T", "matmul", (pb[sb1][0:1, 0:1], c_ones[0:1, 0:1], c_ones[0:1, 0:1]),
         dict(start=True, stop=True, skip_group_check=True),
         reads=("consts",), writes=(PB[bb1], PB[bb2], PB[sb1]))
    prev = None
    for c in range(8):
        cc = cols[c]
        S.op("V", "tensor_tensor",
             (src_sb[:, cc:cc + n], src_sb[:, cc:cc + n], pb[bb1][:, 0:n], ALU.subtract),
             reads=(src_key, PB[bb1]), writes=(src_key,))
        if prev is not None:
            S.op("V", "tensor_tensor",
                 (dst_sb[:, prev[1]:prev[1] + n], src_sb[:, prev[0]:prev[0] + n],
                  pb[bb2][:, 0:n], ALU.mult),
                 reads=(src_key, PB[bb2]), writes=(dst_key,))
        prev = (cc, dst_cols[c])
    S.op("V", "tensor_tensor",
         (dst_sb[:, prev[1]:prev[1] + n], src_sb[:, prev[0]:prev[0] + n],
          pb[bb2][:, 0:n], ALU.mult),
         reads=(src_key, PB[bb2]), writes=(dst_key,))


def build_program(stage=None, debug=None):
    stage = STAGE if stage is None else stage
    debug = DEBUG if debug is None else debug
    nc = bass.Bass("TRN2", target_bir_lowering=False)

    d = {}
    def din(name, shape, dt):
        d[name] = nc.dram_tensor(name, list(shape), dt, kind="ExternalInput")

    din("xT", (E, A), F32)
    din("wqkv_r", (24, 8, 128, 128), F32)
    din("wprojT", (E, E), F32)
    din("wlat2", (128, DH), F32)
    din("wgT", (E, HID), I32)
    din("wuT", (E, HID), I32)
    din("wdT", (HID, E), I32)
    din("bqkv_col", (128, 24), F32)
    din("blat_col", (128, 1), F32)
    din("blat_b512", (128, 512), F32)
    din("bproj_col", (128, 8), F32)
    din("mq0", (128, 488), BF16)
    din("mqs", (128, 488), BF16)
    din("mlast", (128, 96), BF16)
    din("ones_col", (128, 1), BF16)
    din("ones_row_f32", (128, 128), F32)
    din("ones_sel", (128, 256), BF16)
    din("ones128", (128, 128), BF16)
    din("selbc", (128, 16 * 128), BF16)
    din("b29", (128, 1), F32)
    din("ones_col_f32", (128, 1), F32)
    din("ind", (128, 128), BF16)

    x1_hbm = nc.dram_tensor("x1_dbg", [E, TOK], F32, kind="ExternalOutput") \
        if (debug and stage >= 3) else None
    y = nc.dram_tensor("y", [E, TOK], F32, kind="ExternalOutput")
    dbg = {}
    if debug:
        if stage >= 1:
            dbg["q"] = nc.dram_tensor("dbg_q", [128, 8 * A], BF16, kind="ExternalOutput")
            dbg["klat"] = nc.dram_tensor("dbg_klat", [128, 8 * A], BF16, kind="ExternalOutput")
            dbg["vlat"] = nc.dram_tensor("dbg_vlat", [128, NSUB * E], BF16, kind="ExternalOutput")
            dbg["negD"] = nc.dram_tensor("dbg_negD", [128, TOK], BF16, kind="ExternalOutput")
        if stage >= 2:
            dbg["attn"] = nc.dram_tensor("dbg_attn", [128, 8 * TOK], BF16, kind="ExternalOutput")
        if stage >= 4:
            dbg["x1full"] = nc.dram_tensor("dbg_x1full", [128, 16384], BF16, kind="ExternalOutput")
            dbg["z2"] = nc.dram_tensor("dbg_z2", [2, 128, 8192], F32, kind="ExternalOutput")
            dbg["a4"] = nc.dram_tensor("dbg_a4", [2, 128, 24576], BF16, kind="ExternalOutput")
            dbg["P0"] = nc.dram_tensor("dbg_P0", [128, 488], BF16, kind="ExternalOutput")
            dbg["P1"] = nc.dram_tensor("dbg_P1", [128, 488], BF16, kind="ExternalOutput")


    S = Sched()
    ctx = ExitStack()
    EC = ctx.enter_context

    with ctx:
        pb = [EC(nc.psum_tensor(f"pb{i}", [128, 512], F32)) for i in range(8)]
        PB = [f"pb{i}" for i in range(8)]
        for i in range(8):
            S.new_res(PB[i])

        sems = {e: EC(nc.semaphore(f"sem{e}")) for e in ENGS}
        block = EC(nc.Block())

        def finish():
            S.op("Y", "wait_only", extra_deps=list(S.glast.values()))
            for g in S.gcnt:
                sems[g] = EC(nc.semaphore(f"sg_{g}"))
            with nc.allow_low_precision("bf16 matmul pipeline with fp32 accumulation by design"):
                S.emit(block, sems)

        c_bqkv = EC(nc.sbuf_tensor("c_bqkv", [128, 24], F32))
        c_blat = EC(nc.sbuf_tensor("c_blat", [128, 1], F32))
        c_blat512 = EC(nc.sbuf_tensor("c_blat512", [128, 512], F32))
        c_bproj = EC(nc.sbuf_tensor("c_bproj", [128, 8], F32))
        c_mq0 = EC(nc.sbuf_tensor("c_mq0", [128, 488], BF16))
        c_mqs = EC(nc.sbuf_tensor("c_mqs", [128, 488], BF16))
        c_mlast = EC(nc.sbuf_tensor("c_mlast", [128, 96], BF16))
        c_ones = EC(nc.sbuf_tensor("c_ones", [128, 1], BF16))
        c_onesrow = EC(nc.sbuf_tensor("c_onesrow", [128, 128], F32))
        c_osel = EC(nc.sbuf_tensor("c_osel", [128, 256], BF16))
        c_ones128 = EC(nc.sbuf_tensor("c_ones128", [128, 128], BF16))
        c_selbc = EC(nc.sbuf_tensor("c_selbc", [128, 16 * 128], BF16))
        c_b29 = EC(nc.sbuf_tensor("c_b29", [128, 1], F32))
        c_onesf = EC(nc.sbuf_tensor("c_onesf", [128, 1], F32))
        stt_sb = EC(nc.sbuf_tensor("stt_sb", [1, 5 * 512], F32))
        c_ind = EC(nc.sbuf_tensor("c_ind", [128, 128], BF16))
        c_wlat = EC(nc.sbuf_tensor("c_wlat", [128, DH], BF16))
        S.new_res("consts")
        for name, t in [("bqkv_col", c_bqkv), ("blat_col", c_blat), ("blat_b512", c_blat512),
                        ("bproj_col", c_bproj), ("mq0", c_mq0), ("mqs", c_mqs),
                        ("mlast", c_mlast), ("ones_col", c_ones), ("ones_row_f32", c_onesrow),
                        ("ones_sel", c_osel), ("ones128", c_ones128), ("selbc", c_selbc),
                        ("b29", c_b29), ("ones_col_f32", c_onesf), ("ind", c_ind)]:
            S.op("Y", "dma_start", kw=dict(out=t[:, :], in_=d[name].ap()), writes=("consts",),
                 grp="cY")
        S.op("G", "dma_start", kw=dict(out=c_wlat[:, :], in_=d["wlat2"].ap()), writes=("consts",),
             grp="cG")

        def hot(k=1):
            for _ in range(k):
                S.op("T", "matmul", (pb[7][:, 0:512], c_selbc[:, 0:128], c_selbc[:, 0:512]),
                     dict(start=True, stop=True, skip_group_check=True))

        hot(192)

        s12 = ExitStack()
        q_sb = s12.enter_context(nc.sbuf_tensor("q_sb", [128, 8 * A], BF16))
        klat_sb = s12.enter_context(nc.sbuf_tensor("klat_sb", [128, 8 * A], BF16))
        vlat_sb = s12.enter_context(nc.sbuf_tensor("vlat_sb", [128, NSUB * E], BF16))
        qn2_sb = s12.enter_context(nc.sbuf_tensor("qn2_sb", [16, A], F32))
        kn2_sb = s12.enter_context(nc.sbuf_tensor("kn2_sb", [16, A], F32))
        wka_sb = s12.enter_context(nc.sbuf_tensor("wka_sb", [16, A], F32))
        negD_sb = s12.enter_context(nc.sbuf_tensor("negD_sb", [16, TOK], BF16))
        for k in ("q", "klat", "vlat", "qn2", "kn2", "wka", "negD"):
            S.new_res(k)

        s1 = ExitStack()
        xb_sb = [s1.enter_context(nc.sbuf_tensor(f"xb{c}", [128, A], BF16)) for c in range(8)]
        wq_sb = [s1.enter_context(nc.sbuf_tensor(f"wq{i}", [128, 1024], BF16)) for i in range(2)]
        kfm_sb = [s1.enter_context(nc.sbuf_tensor(f"kfm{i}", [128, A], BF16)) for i in range(2)]
        vfm_sb = [s1.enter_context(nc.sbuf_tensor(f"vfm{i}", [128, A], BF16)) for i in range(2)]
        sq_sb = [s1.enter_context(nc.sbuf_tensor(f"sq{i}", [128, 512], BF16)) for i in range(2)]
        for c in range(8):
            S.new_res(f"xb{c}")
        for i in range(2):
            for k in ("wq", "kfm", "vfm", "sq"):
                S.new_res(f"{k}{i}")


        def wq_load(fc):
            slot = fc % 2
            src = d["wqkv_r"].ap()[fc:fc + 1].rearrange("o c p f -> p (o c) f")
            S.op("G", "dma_start",
                 kw=dict(out=wq_sb[slot][:, :].rearrange("p (c f) -> p c f", f=128), in_=src),
                 writes=(f"wq{slot}",), grp=f"wq{slot}")

        wq_load(0)
        wq_load(1)
        for c in range(8):
            S.op("G", "dma_start",
                 kw=dict(out=xb_sb[c][:, :], in_=d["xT"].ap()[128 * c:128 * (c + 1), :]),
                 writes=(f"xb{c}",), grp="xb")

        KE_B, KO_B, VE_B, VO_B, NRM_B = 2, 3, 4, 5, 6

        def norm_pass(src_sb, src_key, dst_sb, dst_key):
            jobsn = [(ri, c) for ri in range(5) for c in range(8)]

            def n_sq(i):
                ri, c = jobsn[i]
                r0, r1 = R5[ri]
                S.op("S", "activation",
                     (sq_sb[i % 2][:, 0:r1 - r0], src_sb[:, A * c + r0:A * c + r1], AF.Square),
                     reads=(src_key,), writes=(f"sq{i % 2}",))

            def n_mm(i):
                ri, c = jobsn[i]
                r0, r1 = R5[ri]
                n = r1 - r0
                S.op("T", "matmul",
                     (pb[NRM_B][0:16, 0:n], c_ind[:, 16 * c:16 * (c + 1)], sq_sb[i % 2][:, 0:n]),
                     dict(start=(c == 0), stop=(c == 7), skip_group_check=True),
                     reads=("consts", f"sq{i % 2}"), writes=(PB[NRM_B],))
                if c == 7:
                    S.op("V", "tensor_copy", (dst_sb[0:16, r0:r1], pb[NRM_B][0:16, 0:n]),
                         reads=(PB[NRM_B],), writes=(dst_key,))

            for i in range(41):
                if i < 40:
                    n_sq(i)
                if i - 1 >= 0:
                    n_mm(i - 1)

        jobs1 = [(fc, ri) for fc in range(24) for ri in range(5)]

        def s1_group(j):
            fc, ri = jobs1[j]
            slot = fc % 2
            r0, r1 = R5[ri]
            n = r1 - r0
            bank = j % 2
            for kc in range(8):
                S.op("T", "matmul",
                     (pb[bank][:, 0:n], wq_sb[slot][:, 128 * kc:128 * (kc + 1)],
                      xb_sb[kc][:, r0:r1]),
                     dict(start=(kc == 0), stop=(kc == 7), skip_group_check=True),
                     reads=(f"wq{slot}", f"xb{kc}"), writes=(PB[bank],))
            if ri == 4 and fc + 2 < 24:
                wq_load(fc + 2)

        def s1_evac(j):
            fc, ri = jobs1[j]
            slot = fc % 2
            r0, r1 = R5[ri]
            n = r1 - r0
            bank = j % 2
            kind = 0 if fc < 8 else (1 if fc < 16 else 2)
            c = fc % 8
            if kind == 0:
                dst_sb, dst_key, dcol = q_sb, "q", A * c
            elif kind == 1:
                dst_sb, dst_key, dcol = kfm_sb[slot], f"kfm{slot}", 0
            else:
                dst_sb, dst_key, dcol = vfm_sb[slot], f"vfm{slot}", 0
            S.op("S", "activation",
                 (dst_sb[:, dcol + r0:dcol + r1], pb[bank][:, 0:n], AF.Identity),
                 dict(bias=c_bqkv[:, fc:fc + 1]),
                 reads=(PB[bank], "consts"), writes=(dst_key,))

        def s1_follow(j):
            fc, ri = jobs1[j]
            slot = fc % 2
            r0, r1 = R5[ri]
            n = r1 - r0
            kind = 0 if fc < 8 else (1 if fc < 16 else 2)
            c = fc % 8
            if kind == 1:
                S.op("T", "matmul",
                     (pb[KE_B][0:64, 0:n], c_wlat[0:64, :], kfm_sb[slot][0:64, r0:r1]),
                     dict(start=True, stop=True, skip_group_check=True),
                     reads=("consts", f"kfm{slot}"), writes=(PB[KE_B],))
                S.op("T", "matmul",
                     (pb[KO_B][64:128, 0:n], c_wlat[64:128, :], kfm_sb[slot][64:128, r0:r1]),
                     dict(start=True, stop=True, skip_group_check=True),
                     reads=("consts", f"kfm{slot}"), writes=(PB[KO_B],))
                S.op("S", "activation",
                     (klat_sb[0:64, A * c + r0:A * c + r1], pb[KE_B][0:64, 0:n], AF.Identity),
                     dict(bias=c_blat[0:64, 0:1]),
                     reads=(PB[KE_B], "consts"), writes=("klat",))
                S.op("S", "activation",
                     (klat_sb[64:128, A * c + r0:A * c + r1], pb[KO_B][64:128, 0:n], AF.Identity),
                     dict(bias=c_blat[64:128, 0:1]),
                     reads=(PB[KO_B], "consts"), writes=("klat",))
            if kind == 2 and ri == 4:
                for sg0, sg1 in ((0, 8), (8, 16), (16, 17)):
                    glen = sg1 - sg0
                    for par, bank in ((0, VE_B), (1, VO_B)):
                        p0 = 64 * par
                        for jj, s in enumerate(range(sg0, sg1)):
                            wlen = 128 if s < 16 else 102
                            S.op("T", "matmul",
                                 (pb[bank][0:wlen, 64 * jj:64 * (jj + 1)],
                                  vfm_sb[slot][p0:p0 + 64, QS * s:QS * s + wlen],
                                  c_wlat[p0:p0 + 64, :]),
                                 dict(start=(jj == 0), stop=(jj == glen - 1), skip_group_check=True),
                                 reads=(f"vfm{slot}", "consts"), writes=(PB[bank],))
                        dst = vlat_sb[:, :].rearrange("p (s e) -> p s e", e=E)[
                            :, sg0:sg1, 128 * c + 64 * par:128 * c + 64 * par + 64]
                        S.op("V", "tensor_tensor",
                             (dst,
                              pb[bank][:, 0:64 * glen].rearrange("p (s e) -> p s e", e=64),
                              c_blat512[:, 0:64 * glen].rearrange("p (s e) -> p s e", e=64),
                              ALU.add),
                             reads=(PB[bank], "consts"), writes=("vlat",))
            if fc == 7 and ri == 4:
                norm_pass(q_sb, "q", qn2_sb, "qn2")
            if fc == 15 and ri == 4:
                norm_pass(klat_sb, "klat", kn2_sb, "kn2")

        for j in range(122):
            if j < 120:
                s1_group(j)
            hot(1)
            if 0 <= j - 1 < 120:
                s1_evac(j - 1)
            if 0 <= j - 2 < 120:
                s1_follow(j - 2)

        S.op("V", "tensor_tensor", (wka_sb[0:16, 0:A - 1], kn2_sb[0:16, 0:A - 1],
                                    kn2_sb[0:16, 1:A], ALU.max),
             reads=("kn2",), writes=("wka",))
        S.op("V", "tensor_tensor", (kn2_sb[0:16, 0:A - 3], wka_sb[0:16, 0:A - 3],
                                    wka_sb[0:16, 2:A - 1], ALU.max),
             reads=("wka", "kn2"), writes=("kn2",))
        S.op("V", "tensor_tensor", (wka_sb[0:16, 0:TOK], kn2_sb[0:16, 0:TOK],
                                    kn2_sb[0:16, 3:TOK + 3], ALU.max),
             reads=("kn2",), writes=("wka",))
        HT = TOK // 2
        for hh in range(2):
            o = HT * hh
            S.op("V", "tensor_scalar", (wka_sb[0:16, o:o + HT], wka_sb[0:16, o:o + HT],
                                        1e-30, None, ALU.add),
                 reads=("wka",), writes=("wka",))
            S.op("V", "tensor_scalar", (qn2_sb[0:16, 3 + o:3 + o + HT],
                                        qn2_sb[0:16, 3 + o:3 + o + HT], 1e-30, None, ALU.add),
                 reads=("qn2",), writes=("qn2",))
        for hh in range(2):
            o = HT * hh
            S.op("S", "activation", (kn2_sb[0:16, o:o + HT], wka_sb[0:16, o:o + HT], AF.Ln),
                 reads=("wka",), writes=("kn2",))
            S.op("S", "activation", (wka_sb[0:16, o:o + HT], qn2_sb[0:16, 3 + o:3 + o + HT],
                                     AF.Ln),
                 reads=("qn2",), writes=("wka",))
        for hh in range(2):
            o = HT * hh
            S.op("V", "tensor_tensor", (qn2_sb[0:16, o:o + HT], wka_sb[0:16, o:o + HT],
                                        kn2_sb[0:16, o:o + HT], ALU.add),
                 reads=("wka", "kn2"), writes=("qn2",))
        for hh in range(2):
            o = HT * hh
            S.op("S", "activation", (wka_sb[0:16, o:o + HT], qn2_sb[0:16, o:o + HT], AF.Exp),
                 dict(scale=0.5), reads=("qn2",), writes=("wka",))
        for hh in range(2):
            o = HT * hh
            S.op("S", "activation", (negD_sb[0:16, o:o + HT], wka_sb[0:16, o:o + HT],
                                     AF.Identity), dict(scale=-1.0),
                 reads=("wka",), writes=("negD",))

        if debug and stage >= 1:
            S.op("Y", "dma_start", kw=dict(out=dbg["q"].ap(), in_=q_sb[:, :]), reads=("q",),
                 grp="dbg")
            S.op("Y", "dma_start", kw=dict(out=dbg["klat"].ap(), in_=klat_sb[:, :]),
                 reads=("klat",), grp="dbg")
            S.op("Y", "dma_start", kw=dict(out=dbg["vlat"].ap(), in_=vlat_sb[:, :]),
                 reads=("vlat",), grp="dbg")
            S.op("Y", "dma_start", kw=dict(out=dbg["negD"].ap()[0:16, :], in_=negD_sb[:, :]),
                 reads=("negD",), grp="dbg")

        s1.close()
        f1 = S.fence()

        if stage < 2:
            s12.close()
            finish()
            return nc

        attn_stk = ExitStack()
        attn_sb = attn_stk.enter_context(
            nc.sbuf_tensor("attn_sb", [128, 8 * TOK], BF16, side="right"))
        S.new_res("attn", fence=f1)
        s2 = ExitStack()
        P_sb = [s2.enter_context(nc.sbuf_tensor(f"P{h}", [128, 488], BF16)) for h in range(16)]
        gbh_sb = [s2.enter_context(nc.sbuf_tensor(f"gbh{i}", [128, 488], BF16)) for i in range(2)]
        lnS_sb = [s2.enter_context(nc.sbuf_tensor(f"lnS{i}", [128, 488], F32)) for i in range(2)]
        for h in range(16):
            S.new_res(f"P{h}", fence=f1)
        for i in range(2):
            S.new_res(f"gbh{i}", fence=f1)
            S.new_res(f"lnS{i}", fence=f1)

        SC_B = (0, 1)
        SUM_B = 2
        AT_B = (3, 4)
        G_B = (5, 6)

        units = []
        for qi, (q0, q1) in enumerate(QUADS):
            subs = list(range(q0, q1))
            w = sum(SUBQ[s] for s in subs)
            mask = c_mq0 if qi == 0 else (c_mqs if qi < 4 else c_mlast)
            for h in range(16):
                units.append((qi, h, subs, w, mask))
        nU = len(units)

        def p_scores(u):
            qi, h, subs, w, mask = units[u]
            c, po = h // 2, 64 * (h % 2)
            sb = SC_B[u % 2]
            col = 0
            for s in subs:
                keys, qn = SUBK[s], SUBQ[s]
                S.op("T", "matmul",
                     (pb[sb][0:keys, col:col + qn],
                      klat_sb[po:po + 64, A * c + QS * s:A * c + QS * s + keys],
                      q_sb[po:po + 64, A * c + 3 + QS * s:A * c + 3 + QS * s + qn]),
                     dict(start=(s == subs[0]), stop=False, skip_group_check=True),
                     reads=("klat", "q"), writes=(PB[sb],))
                col += qn
            S.op("T", "matmul",
                 (pb[sb][0:128, 0:w], c_selbc[0:16, 128 * h:128 * (h + 1)],
                  negD_sb[0:16, QS * subs[0]:QS * subs[0] + w]),
                 dict(start=False, stop=True, skip_group_check=True),
                 reads=("consts", "negD"), writes=(PB[sb],))
            S.op("S", "activation",
                 (P_sb[h][:, 0:w], pb[sb][:, 0:w], AF.Exp), dict(scale=0.125),
                 reads=(PB[sb],), writes=(f"P{h}",))

        def p_mid(u):
            qi, h, subs, w, mask = units[u]
            mb = 2 + (u % 2)
            keys0 = 128 if subs[0] < 16 else 102
            S.op("V", "tensor_tensor",
                 (P_sb[h][:, 0:w], P_sb[h][:, 0:w], mask[:, 0:w], ALU.mult),
                 reads=(f"P{h}", "consts"), writes=(f"P{h}",))
            S.op("T", "matmul",
                 (pb[mb][:, 0:w], c_ones128[0:keys0, :], P_sb[h][0:keys0, 0:w]),
                 dict(start=True, stop=True, skip_group_check=True),
                 reads=("consts", f"P{h}"), writes=(PB[mb],))
            S.op("S", "activation", (lnS_sb[u % 2][:, 0:w], pb[mb][:, 0:w], AF.Ln),
                 reads=(PB[mb],), writes=(f"lnS{u % 2}",))
            S.op("S", "activation", (gbh_sb[u % 2][:, 0:w], lnS_sb[u % 2][:, 0:w], AF.Exp),
                 dict(scale=-1.0, bias=c_b29[:, 0:1]),
                 reads=(f"lnS{u % 2}", "consts"), writes=(f"gbh{u % 2}",))

        def p_out(u):
            qi, h, subs, w, mask = units[u]
            c, po = h // 2, 64 * (h % 2)
            ab = AT_B[u % 2]
            S.op("V", "tensor_tensor",
                 (P_sb[h][:, 0:w], P_sb[h][:, 0:w], gbh_sb[u % 2][:, 0:w], ALU.mult),
                 reads=(f"P{h}", f"gbh{u % 2}"), writes=(f"P{h}",), latch=True)
            col = 0
            for s in subs:
                keys, qn = SUBK[s], SUBQ[s]
                S.op("T", "matmul",
                     (pb[ab][po:po + 64, col:col + qn],
                      vlat_sb[0:keys, E * s + 64 * h:E * s + 64 * (h + 1)],
                      P_sb[h][0:keys, col:col + qn]),
                     dict(start=(s == subs[0]), stop=(s == subs[-1]), skip_group_check=True),
                     reads=("vlat", f"P{h}"), writes=(PB[ab],))
                col += qn
            S.op("V", "tensor_copy",
                 (attn_sb[po:po + 64, TOK * c + QS * subs[0]:TOK * c + QS * subs[0] + w],
                  pb[ab][po:po + 64, 0:w]),
                 reads=(PB[ab],), writes=("attn",))

        for t in range(nU + 2):
            if t < nU:
                p_scores(t)
            hot(1)
            if 0 <= t - 1 < nU:
                p_mid(t - 1)
            if 0 <= t - 2 < nU:
                p_out(t - 2)
                if debug and t - 2 <= 1:
                    qi, h, subs, w, mask = units[t - 2]
                    S.op("Y", "dma_start", kw=dict(out=dbg[f"P{h}"].ap(), in_=P_sb[h][:, :]),
                         reads=(f"P{h}",), grp="dbg")

        if debug and stage >= 2:
            S.op("Y", "dma_start", kw=dict(out=dbg["attn"].ap(), in_=attn_sb[:, :]),
                 reads=("attn",), grp="dbg")

        if stage < 3:
            s2.close()
            s12.close()
            attn_stk.close()
            finish()
            return nc

        s2.close()
        s12.close()
        f2 = S.fence()

        x1full_sb = EC(nc.sbuf_tensor("x1full_sb", [128, 16384], BF16))
        s3 = ExitStack()
        wproj_sb = s3.enter_context(nc.sbuf_tensor("wproj_sb", [128, 8 * 1024], BF16))
        xres_sb = [s3.enter_context(nc.sbuf_tensor(f"xres{i}", [128, 4096], F32)) for i in range(2)]
        z_sb = [s3.enter_context(nc.sbuf_tensor(f"z{i}", [128, 4096], F32)) for i in range(2)]
        zb_sb = [s3.enter_context(nc.sbuf_tensor(f"zb{i}", [128, 4096], BF16)) for i in range(2)]
        zsq_sb = [s3.enter_context(nc.sbuf_tensor(f"zsq{i}", [128, 512], BF16)) for i in range(2)]

        S.new_res("wproj", fence=f2)
        for i in range(2):
            for k in ("xres", "z", "zb", "zsq"):
                S.new_res(f"{k}{i}", fence=f2)
        S.new_res("x1full", fence=f2)
        S.new_res("stt3")

        for kc in range(8):
            S.op("G", "dma_start",
                 kw=dict(out=wproj_sb[:, 1024 * kc:1024 * (kc + 1)],
                         in_=d["wprojT"].ap()[128 * kc:128 * (kc + 1), :]),
                 writes=("wproj",), grp="wproj")

        def s3_proj(r):
            sl = r % 2
            S.op("Y", "dma_start",
                 kw=dict(out=xres_sb[sl][:, :].rearrange("p (c n) -> p c n", n=512),
                         in_=d["xT"].ap()[:, 3 + 512 * r:3 + 512 * (r + 1)].rearrange(
                             "(c p) n -> p c n", p=128)),
                 writes=(f"xres{sl}",), grp=f"xres{sl}")
            for fc in range(8):
                bank = (r * 8 + fc) % 2
                for kc in range(8):
                    S.op("T", "matmul",
                         (pb[bank][:, 0:512],
                          wproj_sb[:, 1024 * kc + 128 * fc:1024 * kc + 128 * (fc + 1)],
                          attn_sb[:, TOK * kc + 512 * r:TOK * kc + 512 * (r + 1)]),
                         dict(start=(kc == 0), stop=(kc == 7), skip_group_check=True),
                         reads=("wproj", "attn"), writes=(PB[bank],))
                S.op("V", "scalar_tensor_tensor",
                     (z_sb[sl][:, 512 * fc:512 * (fc + 1)], pb[bank][:, 0:512],
                      c_bproj[:, fc:fc + 1], xres_sb[sl][:, 512 * fc:512 * (fc + 1)],
                      ALU.add, ALU.add),
                     reads=(PB[bank], "consts", f"xres{sl}"), writes=(f"z{sl}",))
                S.op("S", "activation",
                     (zb_sb[sl][:, 512 * fc:512 * (fc + 1)],
                      z_sb[sl][:, 512 * fc:512 * (fc + 1)], AF.Copy),
                     reads=(f"z{sl}",), writes=(f"zb{sl}",))

        def s3_ln(r):
            sl = r % 2
            _ln(S, pb, PB, c_ones, c_onesrow, stt_sb, "stt3",
                z_sb[sl], f"z{sl}", (zb_sb[sl], [512 * c for c in range(8)]), f"zb{sl}",
                zsq_sb, ("zsq0", "zsq1"),
                x1full_sb, "x1full", 512,
                [512 * c for c in range(8)], (2, 3, 4, 5),
                dst_cols=[2048 * c + 512 * r for c in range(8)])

        for r in range(5):
            if r < 4:
                s3_proj(r)
            hot(2)
            if r - 1 >= 0:
                s3_ln(r - 1)

        if debug and stage >= 3 and x1_hbm is not None:
            for r in range(4):
                S.op("G", "dma_start",
                     kw=dict(out=x1_hbm.ap()[:, 512 * r:512 * (r + 1)].rearrange(
                                 "(c p) n -> p c n", p=128),
                             in_=x1full_sb[:, :].rearrange("p (c n) -> p c n", n=2048)[
                                 :, :, 512 * r:512 * (r + 1)]),
                     reads=("x1full",))
        if stage < 4:
            s3.close()
            attn_stk.close()
            finish()
            return nc

        s3.close()
        attn_stk.close()
        f3 = S.fence()

        s4 = ExitStack()
        z2_sb = s4.enter_context(nc.sbuf_tensor("z2_sb", [128, 8192], F32))
        wg_sb = [s4.enter_context(nc.sbuf_tensor(f"wg{i}", [128, 4096], BF16)) for i in range(2)]
        wu_sb = [s4.enter_context(nc.sbuf_tensor(f"wu{i}", [128, 4096], BF16)) for i in range(2)]
        wd_sb = [s4.enter_context(nc.sbuf_tensor(f"wd{i}", [128, 2048], BF16)) for i in range(3)]
        gsil_sb = [s4.enter_context(nc.sbuf_tensor(f"gsil{i}", [128, 512], BF16)) for i in range(2)]
        ub_sb = [s4.enter_context(nc.sbuf_tensor(f"ub{i}", [128, 512], BF16)) for i in range(2)]
        a_sb = s4.enter_context(nc.sbuf_tensor("a_sb", [128, 24 * 1024], BF16))
        outh_sb = s4.enter_context(nc.sbuf_tensor("outh_sb", [128, 4096], F32))
        zsqb_sb = [s4.enter_context(nc.sbuf_tensor(f"zsqb{i}", [128, 512], BF16)) for i in range(2)]
        for i in range(2):
            for k in ("wg", "wu", "gsil", "ub", "zsqb"):
                S.new_res(f"{k}{i}", fence=f3)
        for i in range(3):
            S.new_res(f"wd{i}", fence=f3)
        for k in ("z2", "a", "outh"):
            S.new_res(k, fence=f3)

        def s4_gateup(r):
            for h in range(6):
                ws = h % 2
                S.op("G", "dma_start",
                     kw=dict(out=wg_sb[ws][:, :].rearrange("p (c f) -> p c f", f=512),
                             in_=d["wgT"].ap()[:, 512 * h:512 * (h + 1)].rearrange(
                                 "(c p) f -> p c f", p=128)),
                     writes=(f"wg{ws}",), grp=f"wg{ws}")
                S.op("G", "dma_start",
                     kw=dict(out=wu_sb[ws][:, :].rearrange("p (c f) -> p c f", f=512),
                             in_=d["wuT"].ap()[:, 512 * h:512 * (h + 1)].rearrange(
                                 "(c p) f -> p c f", p=128)),
                     writes=(f"wu{ws}",), grp=f"wu{ws}")
                for hf in range(4):
                    for half in range(2):
                        gbank = (hf * 2 + half) % 2
                        ubank = 2 + gbank
                        for kc in range(8):
                            rhs = x1full_sb[:, 2048 * kc + 1024 * r + 512 * half:
                                            2048 * kc + 1024 * r + 512 * (half + 1)]
                            S.op("T", "matmul",
                                 (pb[gbank][:, 0:512],
                                  wg_sb[ws][:, 512 * kc + 128 * hf:512 * kc + 128 * (hf + 1)],
                                  rhs),
                                 dict(start=(kc == 0), stop=(kc == 7), skip_group_check=True),
                                 reads=(f"wg{ws}", "x1full"), writes=(PB[gbank],))
                            S.op("T", "matmul",
                                 (pb[ubank][:, 0:512],
                                  wu_sb[ws][:, 512 * kc + 128 * hf:512 * kc + 128 * (hf + 1)],
                                  rhs),
                                 dict(start=(kc == 0), stop=(kc == 7), skip_group_check=True),
                                 reads=(f"wu{ws}", "x1full"), writes=(PB[ubank],))
                        gs = (hf * 2 + half) % 2
                        S.op("S", "activation",
                             (gsil_sb[gs][:, 0:512], pb[gbank][:, 0:512], AF.Silu),
                             dict(scale=INV127),
                             reads=(PB[gbank],), writes=(f"gsil{gs}",))
                        S.op("V", "tensor_scalar",
                             (ub_sb[gs][:, 0:512], pb[ubank][:, 0:512], INV127SQ, None, ALU.mult),
                             reads=(PB[ubank],), writes=(f"ub{gs}",))
                        ahc = 4 * h + hf
                        S.op("V", "tensor_tensor",
                             (a_sb[:, 1024 * ahc + 512 * half:1024 * ahc + 512 * (half + 1)],
                              gsil_sb[gs][:, 0:512], ub_sb[gs][:, 0:512], ALU.mult),
                             reads=(f"gsil{gs}", f"ub{gs}"), writes=("a",))

        def s4_down(r):
            if debug:
                S.op("Y", "dma_start", kw=dict(out=dbg["a4"].ap()[r], in_=a_sb[:, :]),
                     reads=("a",), grp="dbg")
                if r == 0:
                    S.op("Y", "dma_start", kw=dict(out=dbg["x1full"].ap(), in_=x1full_sb[:, :]),
                         reads=("x1full",), grp="dbg")
            for fcg in range(2):
                for hcp in range(12):
                    ws = hcp % 3
                    S.op("G", "dma_start",
                         kw=dict(out=wd_sb[ws][:, :].rearrange("p (c e) -> p c e", e=1024),
                                 in_=d["wdT"].ap()[256 * hcp:256 * (hcp + 1), :].rearrange(
                                     "(c p) e -> p c e", p=128)),
                         writes=(f"wd{ws}",), grp=f"wd{ws}")
                    for hi in range(2):
                        hc = 2 * hcp + hi
                        for fi in range(4):
                            fc = 4 * fcg + fi
                            for half in range(2):
                                bank = fi * 2 + half
                                S.op("T", "matmul",
                                     (pb[bank][:, 0:512],
                                      wd_sb[ws][:, 1024 * hi + 128 * fc:1024 * hi + 128 * (fc + 1)],
                                      a_sb[:, 1024 * hc + 512 * half:1024 * hc + 512 * (half + 1)]),
                                     dict(start=(hc == 0), stop=(hc == 23), skip_group_check=True),
                                     reads=(f"wd{ws}", "a"), writes=(PB[bank],))
                for fi in range(4):
                    fc = 4 * fcg + fi
                    for half in range(2):
                        bank = fi * 2 + half
                        cc = 1024 * fc + 512 * half
                        x1c = 2048 * fc + 1024 * r + 512 * half
                        S.op("V", "tensor_tensor",
                             (z2_sb[:, cc:cc + 512], pb[bank][:, 0:512],
                              x1full_sb[:, x1c:x1c + 512], ALU.add),
                             reads=(PB[bank], "x1full"), writes=("z2",))

        def s4_ln2(r):
            if debug:
                S.op("Y", "dma_start", kw=dict(out=dbg["z2"].ap()[r], in_=z2_sb[:, :]),
                     reads=("z2",), grp="dbg")
            for half in range(2):
                cols = [1024 * c + 512 * half for c in range(8)]
                for c in range(8):
                    S.op("S", "activation",
                         (a_sb[:, cols[c]:cols[c] + 512],
                          z2_sb[:, cols[c]:cols[c] + 512], AF.Copy),
                         reads=("z2",), writes=("a",))
                _ln(S, pb, PB, c_ones, c_onesrow, stt_sb, "stt3",
                    z2_sb, "z2", (a_sb, cols), "a",
                    zsqb_sb, ("zsqb0", "zsqb1"),
                    outh_sb, "outh", 512, cols, (4, 5, 6, 7),
                    dst_cols=[512 * c for c in range(8)])
                S.op("Y", "dma_start",
                     kw=dict(out=y.ap()[:, 1024 * r + 512 * half:1024 * r + 512 * (half + 1)]
                             .rearrange("(c p) n -> p c n", p=128),
                             in_=outh_sb[:, :].rearrange("p (c n) -> p c n", n=512)),
                     reads=("outh",), grp="yst")

        s4_gateup(0)
        s4_down(0)
        s4_ln2(0)
        s4_gateup(1)
        s4_down(1)
        s4_ln2(1)

        s4.close()
        finish()
    return nc



def _window_mask_blocks():
    std = np.zeros((128, 122), dtype=np.float32)
    for t in range(122):
        std[t:t + 7, t] = 1.0
    w0 = std.copy()
    w0[0:3, :] = 0.0
    w0[0, 0], w0[0, 1], w0[0, 2] = 3.0, 2.0, 1.0
    wl = np.zeros((128, 96), dtype=np.float32)
    for t in range(96):
        wl[t:t + 7, t] = 1.0
    wl[102:, :] = 0.0
    wl7 = wl.copy()
    wl7[99:102, :] = 0.0
    wl7[99, 93], wl7[99, 94], wl7[99, 95] = 1.0, 2.0, 3.0
    return std, w0, wl, wl7


_CACHE = {}


def kernel(L0_attn_x, L0_attn_W_qkv, L0_attn_b_qkv, L0_attn_W_lat, L0_attn_b_lat,
           L0_attn_W_proj, L0_attn_b_proj, L0_ffn_w_gate, L0_ffn_w_up, L0_ffn_w_down):
    bf = ml_dtypes.bfloat16
    x = np.ascontiguousarray(np.asarray(L0_attn_x, dtype=np.float32)[0])
    Wqkv = np.asarray(L0_attn_W_qkv, dtype=np.float32)
    bqkv = np.asarray(L0_attn_b_qkv, dtype=np.float32)
    Wlat = np.asarray(L0_attn_W_lat, dtype=np.float32)
    blat = np.asarray(L0_attn_b_lat, dtype=np.float32)
    Wproj = np.asarray(L0_attn_W_proj, dtype=np.float32)
    bproj = np.asarray(L0_attn_b_proj, dtype=np.float32)

    wq_lay = np.ascontiguousarray(Wqkv.T)
    wqkv_r = np.ascontiguousarray(wq_lay.reshape(8, 128, 24, 128).transpose(2, 0, 1, 3))
    wprojT = np.ascontiguousarray(Wproj.T)
    wlat2 = np.ascontiguousarray(np.vstack([Wlat.T, Wlat.T]))
    wgT = np.ascontiguousarray(np.asarray(L0_ffn_w_gate, dtype=np.int32).T)
    wuT = np.ascontiguousarray(np.asarray(L0_ffn_w_up, dtype=np.int32).T)
    wdT = np.ascontiguousarray(np.asarray(L0_ffn_w_down, dtype=np.int32).T)
    bqkv_col = np.ascontiguousarray(bqkv.reshape(24, 128).T)
    blat_col = np.ascontiguousarray(np.concatenate([blat, blat])[:, None])
    blat_b512 = np.ascontiguousarray(np.tile(blat, (128, 8)))
    bproj_col = np.ascontiguousarray(bproj.reshape(8, 128).T)
    ones_col = np.ones((128, 1), dtype=bf)
    ones_row_f32 = np.ones((128, 128), dtype=np.float32)
    ones_sel = np.zeros((128, 256), dtype=np.float32)
    for h in range(16):
        ones_sel[:, 16 * h + h] = 1.0
    ind = np.zeros((128, 128), dtype=np.float32)
    for c in range(8):
        ind[0:64, 16 * c + 2 * c] = 1.0
        ind[64:128, 16 * c + 2 * c + 1] = 1.0
    selbc = np.zeros((128, 16 * 128), dtype=np.float32)
    for h in range(16):
        selbc[h, 128 * h:128 * (h + 1)] = 1.0

    std, w0blk, wl_std, wl7 = _window_mask_blocks()
    mqs = np.zeros((128, 488), dtype=np.float32)
    for j in range(4):
        mqs[:, 122 * j:122 * (j + 1)] = std

    key = ("prog", STAGE, DEBUG)
    if key not in _CACHE:
        _CACHE[key] = build_program(STAGE, DEBUG)
    nc = _CACHE[key]

    in_maps = []
    for c in range(NCORE):
        base = 2048 * c
        rows = np.arange(base - 3, base + 2051)
        if c == 0:
            rows = rows.copy()
            rows[0] = SEQ - 1
            rows[1] = -1
            rows[2] = -1
        if c == NCORE - 1:
            rows = rows.copy()
            rows[2051] = 0
            rows[2052] = -1
            rows[2053] = -1
        xc = np.zeros((A, E), dtype=np.float32)
        valid = rows >= 0
        xc[valid] = x[rows[valid]]
        xT = np.ascontiguousarray(xc.T)

        mq0 = mqs.copy()
        mq0[:, 0:122] = w0blk if c == 0 else std
        mlast = wl7 if c == NCORE - 1 else wl_std

        in_maps.append(dict(
            xT=xT, wqkv_r=wqkv_r, wprojT=wprojT, wlat2=wlat2,
            wgT=wgT, wuT=wuT, wdT=wdT,
            bqkv_col=bqkv_col, blat_col=blat_col, blat_b512=blat_b512,
            bproj_col=bproj_col,
            mq0=mq0.astype(bf), mqs=mqs.astype(bf), mlast=mlast.astype(bf),
            ones_col=ones_col, ones_row_f32=ones_row_f32,
            ones_sel=ones_sel.astype(bf), ones128=np.full((128, 128), 2.0**29, dtype=bf),
            b29=np.full((128, 1), 29.0 * np.log(2.0), dtype=np.float32),
            ones_col_f32=np.ones((128, 1), dtype=np.float32),
            selbc=selbc.astype(bf), ind=ind.astype(bf),
        ))

    res = run_bass_kernel_spmd(nc, in_maps, core_ids=list(range(NCORE)))
    out = np.empty((1, SEQ, E), dtype=np.float32)
    for c in range(NCORE):
        out[0, 2048 * c:2048 * (c + 1), :] = np.asarray(res.results[c]["y"]).T
    if DEBUG:
        kernel._last = res
    return out


# version 12
# speedup vs baseline: 1.0305x; 1.0305x over previous
import numpy as np
import ml_dtypes
from contextlib import ExitStack

import concourse.bass as bass
import concourse.mybir as mybir
from concourse.bass_utils import run_bass_kernel_spmd

F32 = mybir.dt.float32
BF16 = mybir.dt.bfloat16
I32 = mybir.dt.int32
AF = mybir.ActivationFunctionType
ALU = mybir.AluOpType
AX = mybir.AxisListType

NCORE = 8
SEQ = 16384
E = 1024
NH = 16
DH = 64
HID = 3072
TOK = 2048
A = TOK + 6
QS = 122
NSUB = 17
SUBQ = [122] * 16 + [96]
SUBK = [128] * 16 + [102]
QUADS = [(0, 4), (4, 8), (8, 12), (12, 16), (16, 17)]
R5 = [(0, 512), (512, 1024), (1024, 1536), (1536, 2048), (2048, 2054)]
INV127 = 1.0 / 127.0
INV127SQ = INV127 * INV127

STAGE = 4
DEBUG = False
DBGQ = 0

ENGS = ("T", "V", "S", "Y", "G")
BLOCK_ATTR = {"T": "tensor", "V": "vector", "S": "scalar", "Y": "sync", "G": "gpsimd"}
DMA_ENGS = ("Y", "G")


class _Op:
    __slots__ = ("eng", "name", "args", "kw", "deps", "gdeps", "inc", "cum", "idx", "fragile",
                 "grp")


class Sched:
    def __init__(self):
        self.ops = {e: [] for e in ENGS}
        self.res = {}
        self.gcnt = {}
        self.glast = {}

    def new_res(self, key, fence=None):
        self.res[key] = {"w": None, "r": [], "f": list(fence or [])}

    def fence(self):
        return [(e, len(self.ops[e]) - 1) for e in ENGS if self.ops[e]]

    def op(self, eng, name, args=(), kw=None, reads=(), writes=(), extra_deps=(), latch=False,
           fragile=False, grp=None):
        idx = len(self.ops[eng])
        deps = {}
        gdeps = {}

        def add(d):
            de, di = d
            if di < 0 or (de == eng and di >= idx):
                return
            t = self.ops[de][di]
            if de in DMA_ENGS and t.name == "dma_start":
                g = t.grp
                c = self.gcnt[g]
                if gdeps.get(g, 0) < c:
                    gdeps[g] = c
                return
            if de in DMA_ENGS:
                return
            if de == eng and not latch and not (idx - di <= 2):
                return
            if deps.get(de, -1) < di:
                deps[de] = di

        for k in reads:
            r = self.res[k]
            for dd in r["f"]:
                add(dd)
            r["f"] = []
            if r["w"]:
                add(r["w"])
            r["r"].append((eng, idx))
        for k in writes:
            r = self.res[k]
            for dd in r["f"]:
                add(dd)
            r["f"] = []
            if r["w"]:
                add(r["w"])
            for dd in r["r"]:
                add(dd)
            r["r"] = []
            r["w"] = (eng, idx)
        for dd in extra_deps:
            add(dd)
        o = _Op()
        o.eng, o.name, o.args, o.kw = eng, name, args, (kw or {})
        o.deps, o.gdeps, o.inc, o.idx = deps, gdeps, False, idx
        o.fragile = fragile
        o.grp = None
        if eng in DMA_ENGS and name == "dma_start":
            assert grp is not None, "dma_start needs a semaphore group"
            o.grp = grp
            self.gcnt[grp] = self.gcnt.get(grp, 0) + 1
            self.glast[grp] = (eng, idx)
        for de, di in deps.items():
            self.ops[de][di].inc = True
        self.ops[eng].append(o)
        return (eng, idx)

    def finalize(self):
        for e in ENGS:
            c = 0
            for o in self.ops[e]:
                if o.grp is None and o.inc:
                    c += 1
                o.cum = c

    def emit(self, block, sems):
        self.finalize()

        def runner(e):
            def run(engobj):
                mw = {}
                for o in self.ops[e]:
                    for de in sorted(o.deps):
                        di = o.deps[de]
                        val = self.ops[de][di].cum
                        assert val > 0, f"{e} op {o.name} depends on un-inc'd {de}[{di}]"
                        if mw.get(de, -1) >= val:
                            continue
                        mw[de] = val
                        engobj.wait_ge(sems[de], val)
                    for g in sorted(o.gdeps):
                        val = 16 * o.gdeps[g]
                        if mw.get(g, -1) >= val:
                            continue
                        mw[g] = val
                        engobj.wait_ge(sems[g], val)
                    if o.name == "wait_only":
                        continue
                    inst = getattr(engobj, o.name)(*o.args, **o.kw)
                    if o.grp is not None:
                        inst.then_inc(sems[o.grp], 16)
                    elif o.inc:
                        inst.then_inc(sems[e], 1)
            return run

        for e in ENGS:
            getattr(block, BLOCK_ATTR[e])(runner(e))


def _ln(S, pb, PB, c_ones, c_onesrow, stt, stt_key, src_sb, src_key, zb, zb_key,
        zsq, zsq_keys, dst_sb, dst_key, n, cols, banks, dst_cols=None):
    if dst_cols is None:
        dst_cols = cols
    """LayerNorm over the feature dim (8 chunks x 128 partitions) for n tokens.

    zb must hold a bf16 copy of src (stats operand). Overwrites src in place
    (src - mean); writes the normalized fp32 result into dst.
    """
    sb1, sb2, bb1, bb2 = banks
    m_sb = stt[0:1, 0:n]
    e2_sb = stt[0:1, 512:512 + n]
    m2_sb = stt[0:1, 1024:1024 + n]
    var_sb = stt[0:1, 1536:1536 + n]
    lnv_sb = stt[0:1, 1024:1024 + n]
    rstd_sb = stt[0:1, 2048:2048 + n]
    for c in range(8):
        S.op("T", "matmul", (pb[sb1][0:1, 0:n], c_ones[:, 0:1], zb[0][:, zb[1][c]:zb[1][c] + n]),
             dict(start=(c == 0), stop=(c == 7), skip_group_check=True),
             reads=("consts", zb_key), writes=(PB[sb1],))
    for c in range(8):
        zs = zsq[c % 2]
        S.op("S", "activation", (zs[:, 0:n], zb[0][:, zb[1][c]:zb[1][c] + n], AF.Square),
             reads=(zb_key,), writes=(zsq_keys[c % 2],))
        S.op("T", "matmul", (pb[sb2][0:1, 0:n], c_ones[:, 0:1], zs[:, 0:n]),
             dict(start=(c == 0), stop=(c == 7), skip_group_check=True),
             reads=("consts", zsq_keys[c % 2]), writes=(PB[sb2],))
    S.op("V", "tensor_scalar", (m_sb, pb[sb1][0:1, 0:n], 1.0 / E, None, ALU.mult),
         reads=(PB[sb1],), writes=(stt_key,))
    S.op("V", "tensor_scalar", (e2_sb, pb[sb2][0:1, 0:n], 1.0 / E, None, ALU.mult),
         reads=(PB[sb2],), writes=(stt_key,))
    S.op("V", "tensor_tensor", (m2_sb, m_sb, m_sb, ALU.mult), reads=(stt_key,), writes=(stt_key,))
    S.op("V", "tensor_tensor", (var_sb, e2_sb, m2_sb, ALU.subtract), reads=(stt_key,), writes=(stt_key,))
    S.op("V", "tensor_scalar", (var_sb, var_sb, 1e-5, None, ALU.add),
         reads=(stt_key,), writes=(stt_key,))
    S.op("S", "activation", (lnv_sb, var_sb, AF.Ln),
         reads=(stt_key,), writes=(stt_key,))
    S.op("S", "activation", (rstd_sb, lnv_sb, AF.Exp), dict(scale=-0.5),
         reads=(stt_key,), writes=(stt_key,))
    S.op("T", "matmul", (pb[bb1][:, 0:n], c_onesrow[0:1, :], m_sb),
         dict(start=True, stop=True, skip_group_check=True),
         reads=("consts", stt_key), writes=(PB[bb1],))
    S.op("T", "matmul", (pb[bb2][:, 0:n], c_onesrow[0:1, :], rstd_sb),
         dict(start=True, stop=True, skip_group_check=True),
         reads=("consts", stt_key), writes=(PB[bb2],))
    S.op("T", "matmul", (pb[sb1][0:1, 0:1], c_ones[0:1, 0:1], c_ones[0:1, 0:1]),
         dict(start=True, stop=True, skip_group_check=True),
         reads=("consts",), writes=(PB[bb1], PB[bb2], PB[sb1]))
    prev = None
    for c in range(8):
        cc = cols[c]
        S.op("V", "tensor_tensor",
             (src_sb[:, cc:cc + n], src_sb[:, cc:cc + n], pb[bb1][:, 0:n], ALU.subtract),
             reads=(src_key, PB[bb1]), writes=(src_key,))
        if prev is not None:
            S.op("V", "tensor_tensor",
                 (dst_sb[:, prev[1]:prev[1] + n], src_sb[:, prev[0]:prev[0] + n],
                  pb[bb2][:, 0:n], ALU.mult),
                 reads=(src_key, PB[bb2]), writes=(dst_key,))
        prev = (cc, dst_cols[c])
    S.op("V", "tensor_tensor",
         (dst_sb[:, prev[1]:prev[1] + n], src_sb[:, prev[0]:prev[0] + n],
          pb[bb2][:, 0:n], ALU.mult),
         reads=(src_key, PB[bb2]), writes=(dst_key,))


def build_program(stage=None, debug=None):
    stage = STAGE if stage is None else stage
    debug = DEBUG if debug is None else debug
    nc = bass.Bass("TRN2", target_bir_lowering=False)

    d = {}
    def din(name, shape, dt):
        d[name] = nc.dram_tensor(name, list(shape), dt, kind="ExternalInput")

    din("xT", (E, A), F32)
    din("wqkv_r", (24, 8, 128, 128), F32)
    din("wprojT", (E, E), F32)
    din("wlat2", (128, DH), F32)
    din("wgT", (E, HID), I32)
    din("wuT", (E, HID), I32)
    din("wdT", (HID, E), I32)
    din("bqkv_col", (128, 24), F32)
    din("blat_col", (128, 1), F32)
    din("blat_b512", (128, 512), F32)
    din("bproj_col", (128, 8), F32)
    din("mq0", (128, 488), BF16)
    din("mqs", (128, 488), BF16)
    din("mlast", (128, 96), BF16)
    din("ones_col", (128, 1), BF16)
    din("ones_row_f32", (128, 128), F32)
    din("ones_sel", (128, 256), BF16)
    din("ones128", (128, 128), BF16)
    din("selbc", (128, 16 * 128), BF16)
    din("b29", (128, 1), F32)
    din("ones_col_f32", (128, 1), F32)
    din("ind", (128, 128), BF16)

    x1_hbm = nc.dram_tensor("x1_dbg", [E, TOK], F32, kind="ExternalOutput") \
        if (debug and stage >= 3) else None
    y = nc.dram_tensor("y", [E, TOK], F32, kind="ExternalOutput")
    dbg = {}
    if debug:
        if stage >= 1:
            dbg["q"] = nc.dram_tensor("dbg_q", [128, 8 * A], BF16, kind="ExternalOutput")
            dbg["klat"] = nc.dram_tensor("dbg_klat", [128, 8 * A], BF16, kind="ExternalOutput")
            dbg["vlat"] = nc.dram_tensor("dbg_vlat", [128, NSUB * E], BF16, kind="ExternalOutput")
            dbg["negD"] = nc.dram_tensor("dbg_negD", [128, TOK], BF16, kind="ExternalOutput")
        if stage >= 2:
            dbg["attn"] = nc.dram_tensor("dbg_attn", [128, 8 * TOK], BF16, kind="ExternalOutput")
        if stage >= 4:
            dbg["x1full"] = nc.dram_tensor("dbg_x1full", [128, 16384], BF16, kind="ExternalOutput")
            dbg["z2"] = nc.dram_tensor("dbg_z2", [2, 128, 8192], F32, kind="ExternalOutput")
            dbg["a4"] = nc.dram_tensor("dbg_a4", [2, 128, 24576], BF16, kind="ExternalOutput")
            dbg["P0"] = nc.dram_tensor("dbg_P0", [128, 488], BF16, kind="ExternalOutput")
            dbg["P1"] = nc.dram_tensor("dbg_P1", [128, 488], BF16, kind="ExternalOutput")


    S = Sched()
    ctx = ExitStack()
    EC = ctx.enter_context

    with ctx:
        pb = [EC(nc.psum_tensor(f"pb{i}", [128, 512], F32)) for i in range(8)]
        PB = [f"pb{i}" for i in range(8)]
        for i in range(8):
            S.new_res(PB[i])

        sems = {e: EC(nc.semaphore(f"sem{e}")) for e in ENGS}
        block = EC(nc.Block())

        def finish():
            S.op("Y", "wait_only", extra_deps=list(S.glast.values()))
            for g in S.gcnt:
                sems[g] = EC(nc.semaphore(f"sg_{g}"))
            with nc.allow_low_precision("bf16 matmul pipeline with fp32 accumulation by design"):
                S.emit(block, sems)

        c_bqkv = EC(nc.sbuf_tensor("c_bqkv", [128, 24], F32))
        c_blat = EC(nc.sbuf_tensor("c_blat", [128, 1], F32))
        c_blat512 = EC(nc.sbuf_tensor("c_blat512", [128, 512], F32))
        c_bproj = EC(nc.sbuf_tensor("c_bproj", [128, 8], F32))
        c_mq0 = EC(nc.sbuf_tensor("c_mq0", [128, 488], BF16))
        c_mqs = EC(nc.sbuf_tensor("c_mqs", [128, 488], BF16))
        c_mlast = EC(nc.sbuf_tensor("c_mlast", [128, 96], BF16))
        c_ones = EC(nc.sbuf_tensor("c_ones", [128, 1], BF16))
        c_onesrow = EC(nc.sbuf_tensor("c_onesrow", [128, 128], F32))
        c_osel = EC(nc.sbuf_tensor("c_osel", [128, 256], BF16))
        c_ones128 = EC(nc.sbuf_tensor("c_ones128", [128, 128], BF16))
        c_selbc = EC(nc.sbuf_tensor("c_selbc", [128, 16 * 128], BF16))
        c_b29 = EC(nc.sbuf_tensor("c_b29", [128, 1], F32))
        c_onesf = EC(nc.sbuf_tensor("c_onesf", [128, 1], F32))
        stt_sb = EC(nc.sbuf_tensor("stt_sb", [1, 5 * 512], F32))
        c_ind = EC(nc.sbuf_tensor("c_ind", [128, 128], BF16))
        c_wlat = EC(nc.sbuf_tensor("c_wlat", [128, DH], BF16))
        S.new_res("consts")
        for name, t in [("bqkv_col", c_bqkv), ("blat_col", c_blat), ("blat_b512", c_blat512),
                        ("bproj_col", c_bproj), ("mq0", c_mq0), ("mqs", c_mqs),
                        ("mlast", c_mlast), ("ones_col", c_ones), ("ones_row_f32", c_onesrow),
                        ("ones_sel", c_osel), ("ones128", c_ones128), ("selbc", c_selbc),
                        ("b29", c_b29), ("ones_col_f32", c_onesf), ("ind", c_ind)]:
            S.op("Y", "dma_start", kw=dict(out=t[:, :], in_=d[name].ap()), writes=("consts",),
                 grp="cY")
        S.op("G", "dma_start", kw=dict(out=c_wlat[:, :], in_=d["wlat2"].ap()), writes=("consts",),
             grp="cG")

        def hot(k=1):
            for _ in range(k):
                S.op("T", "matmul", (pb[7][:, 0:512], c_selbc[:, 0:128], c_selbc[:, 0:512]),
                     dict(start=True, stop=True, skip_group_check=True), reads=("consts",))

        hot(192)

        s12 = ExitStack()
        q_sb = s12.enter_context(nc.sbuf_tensor("q_sb", [128, 8 * A], BF16))
        klat_sb = s12.enter_context(nc.sbuf_tensor("klat_sb", [128, 8 * A], BF16))
        vlat_sb = s12.enter_context(nc.sbuf_tensor("vlat_sb", [128, NSUB * E], BF16))
        qn2_sb = s12.enter_context(nc.sbuf_tensor("qn2_sb", [16, A], F32))
        kn2_sb = s12.enter_context(nc.sbuf_tensor("kn2_sb", [16, A], F32))
        wka_sb = s12.enter_context(nc.sbuf_tensor("wka_sb", [16, A], F32))
        negD_sb = s12.enter_context(nc.sbuf_tensor("negD_sb", [16, TOK], BF16))
        for k in ("q", "klat", "vlat", "qn2", "kn2", "wka", "negD"):
            S.new_res(k)

        s1 = ExitStack()
        xb_sb = [s1.enter_context(nc.sbuf_tensor(f"xb{c}", [128, A], BF16)) for c in range(8)]
        wq_sb = [s1.enter_context(nc.sbuf_tensor(f"wq{i}", [128, 1024], BF16)) for i in range(2)]
        kfm_sb = [s1.enter_context(nc.sbuf_tensor(f"kfm{i}", [128, A], BF16)) for i in range(2)]
        vfm_sb = [s1.enter_context(nc.sbuf_tensor(f"vfm{i}", [128, A], BF16)) for i in range(2)]
        sq_sb = [s1.enter_context(nc.sbuf_tensor(f"sq{i}", [128, 512], BF16)) for i in range(2)]
        for c in range(8):
            S.new_res(f"xb{c}")
        for i in range(2):
            for k in ("wq", "kfm", "vfm", "sq"):
                S.new_res(f"{k}{i}")


        def wq_load(fc):
            slot = fc % 2
            src = d["wqkv_r"].ap()[fc:fc + 1].rearrange("o c p f -> p (o c) f")
            S.op("G", "dma_start",
                 kw=dict(out=wq_sb[slot][:, :].rearrange("p (c f) -> p c f", f=128), in_=src),
                 writes=(f"wq{slot}",), grp=f"wq{slot}")

        wq_load(0)
        wq_load(1)
        for c in range(8):
            S.op("G", "dma_start",
                 kw=dict(out=xb_sb[c][:, :], in_=d["xT"].ap()[128 * c:128 * (c + 1), :]),
                 writes=(f"xb{c}",), grp="xb")

        KE_B, KO_B, VE_B, VO_B, NRM_B = 2, 3, 4, 5, 6

        def norm_pass(src_sb, src_key, dst_sb, dst_key):
            jobsn = [(ri, c) for ri in range(5) for c in range(8)]

            def n_sq(i):
                ri, c = jobsn[i]
                r0, r1 = R5[ri]
                S.op("S", "activation",
                     (sq_sb[i % 2][:, 0:r1 - r0], src_sb[:, A * c + r0:A * c + r1], AF.Square),
                     reads=(src_key,), writes=(f"sq{i % 2}",))

            def n_mm(i):
                ri, c = jobsn[i]
                r0, r1 = R5[ri]
                n = r1 - r0
                S.op("T", "matmul",
                     (pb[NRM_B][0:16, 0:n], c_ind[:, 16 * c:16 * (c + 1)], sq_sb[i % 2][:, 0:n]),
                     dict(start=(c == 0), stop=(c == 7), skip_group_check=True),
                     reads=("consts", f"sq{i % 2}"), writes=(PB[NRM_B],))
                if c == 7:
                    S.op("V", "tensor_copy", (dst_sb[0:16, r0:r1], pb[NRM_B][0:16, 0:n]),
                         reads=(PB[NRM_B],), writes=(dst_key,))

            for i in range(41):
                if i < 40:
                    n_sq(i)
                if i - 1 >= 0:
                    n_mm(i - 1)

        jobs1 = [(fc, ri) for fc in range(24) for ri in range(5)]

        def s1_group(j):
            fc, ri = jobs1[j]
            slot = fc % 2
            r0, r1 = R5[ri]
            n = r1 - r0
            bank = j % 2
            for kc in range(8):
                S.op("T", "matmul",
                     (pb[bank][:, 0:n], wq_sb[slot][:, 128 * kc:128 * (kc + 1)],
                      xb_sb[kc][:, r0:r1]),
                     dict(start=(kc == 0), stop=(kc == 7), skip_group_check=True),
                     reads=(f"wq{slot}", f"xb{kc}"), writes=(PB[bank],))
            if ri == 4 and fc + 2 < 24:
                wq_load(fc + 2)

        def s1_evac(j):
            fc, ri = jobs1[j]
            slot = fc % 2
            r0, r1 = R5[ri]
            n = r1 - r0
            bank = j % 2
            kind = 0 if fc < 8 else (1 if fc < 16 else 2)
            c = fc % 8
            if kind == 0:
                dst_sb, dst_key, dcol = q_sb, "q", A * c
            elif kind == 1:
                dst_sb, dst_key, dcol = kfm_sb[slot], f"kfm{slot}", 0
            else:
                dst_sb, dst_key, dcol = vfm_sb[slot], f"vfm{slot}", 0
            S.op("S", "activation",
                 (dst_sb[:, dcol + r0:dcol + r1], pb[bank][:, 0:n], AF.Identity),
                 dict(bias=c_bqkv[:, fc:fc + 1]),
                 reads=(PB[bank], "consts"), writes=(dst_key,))

        def s1_follow(j):
            fc, ri = jobs1[j]
            slot = fc % 2
            r0, r1 = R5[ri]
            n = r1 - r0
            kind = 0 if fc < 8 else (1 if fc < 16 else 2)
            c = fc % 8
            if kind == 1:
                S.op("T", "matmul",
                     (pb[KE_B][0:64, 0:n], c_wlat[0:64, :], kfm_sb[slot][0:64, r0:r1]),
                     dict(start=True, stop=True, skip_group_check=True),
                     reads=("consts", f"kfm{slot}"), writes=(PB[KE_B],))
                S.op("T", "matmul",
                     (pb[KO_B][64:128, 0:n], c_wlat[64:128, :], kfm_sb[slot][64:128, r0:r1]),
                     dict(start=True, stop=True, skip_group_check=True),
                     reads=("consts", f"kfm{slot}"), writes=(PB[KO_B],))
                S.op("S", "activation",
                     (klat_sb[0:64, A * c + r0:A * c + r1], pb[KE_B][0:64, 0:n], AF.Identity),
                     dict(bias=c_blat[0:64, 0:1]),
                     reads=(PB[KE_B], "consts"), writes=("klat",))
                S.op("S", "activation",
                     (klat_sb[64:128, A * c + r0:A * c + r1], pb[KO_B][64:128, 0:n], AF.Identity),
                     dict(bias=c_blat[64:128, 0:1]),
                     reads=(PB[KO_B], "consts"), writes=("klat",))
            if kind == 2 and ri == 4:
                for sg0, sg1 in ((0, 8), (8, 16), (16, 17)):
                    glen = sg1 - sg0
                    for par, bank in ((0, VE_B), (1, VO_B)):
                        p0 = 64 * par
                        for jj, s in enumerate(range(sg0, sg1)):
                            wlen = 128 if s < 16 else 102
                            S.op("T", "matmul",
                                 (pb[bank][0:wlen, 64 * jj:64 * (jj + 1)],
                                  vfm_sb[slot][p0:p0 + 64, QS * s:QS * s + wlen],
                                  c_wlat[p0:p0 + 64, :]),
                                 dict(start=(jj == 0), stop=(jj == glen - 1), skip_group_check=True),
                                 reads=(f"vfm{slot}", "consts"), writes=(PB[bank],))
                        dst = vlat_sb[:, :].rearrange("p (s e) -> p s e", e=E)[
                            :, sg0:sg1, 128 * c + 64 * par:128 * c + 64 * par + 64]
                        S.op("V", "tensor_tensor",
                             (dst,
                              pb[bank][:, 0:64 * glen].rearrange("p (s e) -> p s e", e=64),
                              c_blat512[:, 0:64 * glen].rearrange("p (s e) -> p s e", e=64),
                              ALU.add),
                             reads=(PB[bank], "consts"), writes=("vlat",))
            if fc == 7 and ri == 4:
                norm_pass(q_sb, "q", qn2_sb, "qn2")
            if fc == 15 and ri == 4:
                norm_pass(klat_sb, "klat", kn2_sb, "kn2")

        for j in range(122):
            if j < 120:
                s1_group(j)
            if 0 <= j - 1 < 120:
                s1_evac(j - 1)
            if 0 <= j - 2 < 120:
                s1_follow(j - 2)

        S.op("V", "tensor_tensor", (wka_sb[0:16, 0:A - 1], kn2_sb[0:16, 0:A - 1],
                                    kn2_sb[0:16, 1:A], ALU.max),
             reads=("kn2",), writes=("wka",))
        S.op("V", "tensor_tensor", (kn2_sb[0:16, 0:A - 3], wka_sb[0:16, 0:A - 3],
                                    wka_sb[0:16, 2:A - 1], ALU.max),
             reads=("wka", "kn2"), writes=("kn2",))
        S.op("V", "tensor_tensor", (wka_sb[0:16, 0:TOK], kn2_sb[0:16, 0:TOK],
                                    kn2_sb[0:16, 3:TOK + 3], ALU.max),
             reads=("kn2",), writes=("wka",))
        HT = TOK // 2
        for hh in range(2):
            o = HT * hh
            S.op("V", "tensor_scalar", (wka_sb[0:16, o:o + HT], wka_sb[0:16, o:o + HT],
                                        1e-30, None, ALU.add),
                 reads=("wka",), writes=("wka",))
            S.op("V", "tensor_scalar", (qn2_sb[0:16, 3 + o:3 + o + HT],
                                        qn2_sb[0:16, 3 + o:3 + o + HT], 1e-30, None, ALU.add),
                 reads=("qn2",), writes=("qn2",))
        for hh in range(2):
            o = HT * hh
            S.op("S", "activation", (kn2_sb[0:16, o:o + HT], wka_sb[0:16, o:o + HT], AF.Ln),
                 reads=("wka",), writes=("kn2",))
            S.op("S", "activation", (wka_sb[0:16, o:o + HT], qn2_sb[0:16, 3 + o:3 + o + HT],
                                     AF.Ln),
                 reads=("qn2",), writes=("wka",))
        for hh in range(2):
            o = HT * hh
            S.op("V", "tensor_tensor", (qn2_sb[0:16, o:o + HT], wka_sb[0:16, o:o + HT],
                                        kn2_sb[0:16, o:o + HT], ALU.add),
                 reads=("wka", "kn2"), writes=("qn2",))
        for hh in range(2):
            o = HT * hh
            S.op("S", "activation", (wka_sb[0:16, o:o + HT], qn2_sb[0:16, o:o + HT], AF.Exp),
                 dict(scale=0.5), reads=("qn2",), writes=("wka",))
        for hh in range(2):
            o = HT * hh
            S.op("S", "activation", (negD_sb[0:16, o:o + HT], wka_sb[0:16, o:o + HT],
                                     AF.Identity), dict(scale=-1.0),
                 reads=("wka",), writes=("negD",))

        if debug and stage >= 1:
            S.op("Y", "dma_start", kw=dict(out=dbg["q"].ap(), in_=q_sb[:, :]), reads=("q",),
                 grp="dbg")
            S.op("Y", "dma_start", kw=dict(out=dbg["klat"].ap(), in_=klat_sb[:, :]),
                 reads=("klat",), grp="dbg")
            S.op("Y", "dma_start", kw=dict(out=dbg["vlat"].ap(), in_=vlat_sb[:, :]),
                 reads=("vlat",), grp="dbg")
            S.op("Y", "dma_start", kw=dict(out=dbg["negD"].ap()[0:16, :], in_=negD_sb[:, :]),
                 reads=("negD",), grp="dbg")

        s1.close()
        f1 = S.fence()

        if stage < 2:
            s12.close()
            finish()
            return nc

        attn_stk = ExitStack()
        attn_sb = attn_stk.enter_context(
            nc.sbuf_tensor("attn_sb", [128, 8 * TOK], BF16, side="right"))
        S.new_res("attn", fence=f1)
        s2 = ExitStack()
        P_sb = [s2.enter_context(nc.sbuf_tensor(f"P{h}", [128, 488], BF16)) for h in range(16)]
        gbh_sb = [s2.enter_context(nc.sbuf_tensor(f"gbh{i}", [128, 488], BF16)) for i in range(2)]
        lnS_sb = [s2.enter_context(nc.sbuf_tensor(f"lnS{i}", [128, 488], F32)) for i in range(2)]
        for h in range(16):
            S.new_res(f"P{h}", fence=f1)
        for i in range(2):
            S.new_res(f"gbh{i}", fence=f1)
            S.new_res(f"lnS{i}", fence=f1)

        SC_B = (0, 1)
        SUM_B = 2
        AT_B = (3, 4)
        G_B = (5, 6)

        units = []
        for qi, (q0, q1) in enumerate(QUADS):
            subs = list(range(q0, q1))
            w = sum(SUBQ[s] for s in subs)
            mask = c_mq0 if qi == 0 else (c_mqs if qi < 4 else c_mlast)
            for h in range(16):
                units.append((qi, h, subs, w, mask))
        nU = len(units)

        def p_scores(u):
            qi, h, subs, w, mask = units[u]
            c, po = h // 2, 64 * (h % 2)
            sb = SC_B[u % 2]
            col = 0
            for s in subs:
                keys, qn = SUBK[s], SUBQ[s]
                S.op("T", "matmul",
                     (pb[sb][0:keys, col:col + qn],
                      klat_sb[po:po + 64, A * c + QS * s:A * c + QS * s + keys],
                      q_sb[po:po + 64, A * c + 3 + QS * s:A * c + 3 + QS * s + qn]),
                     dict(start=(s == subs[0]), stop=False, skip_group_check=True),
                     reads=("klat", "q"), writes=(PB[sb],))
                col += qn
            S.op("T", "matmul",
                 (pb[sb][0:128, 0:w], c_selbc[0:16, 128 * h:128 * (h + 1)],
                  negD_sb[0:16, QS * subs[0]:QS * subs[0] + w]),
                 dict(start=False, stop=True, skip_group_check=True),
                 reads=("consts", "negD"), writes=(PB[sb],))
            S.op("S", "activation",
                 (P_sb[h][:, 0:w], pb[sb][:, 0:w], AF.Exp), dict(scale=0.125),
                 reads=(PB[sb],), writes=(f"P{h}",))

        def p_mid(u):
            qi, h, subs, w, mask = units[u]
            mb = 2 + (u % 2)
            keys0 = 128 if subs[0] < 16 else 102
            S.op("V", "tensor_tensor",
                 (P_sb[h][:, 0:w], P_sb[h][:, 0:w], mask[:, 0:w], ALU.mult),
                 reads=(f"P{h}", "consts"), writes=(f"P{h}",))
            S.op("T", "matmul",
                 (pb[mb][:, 0:w], c_ones128[0:keys0, :], P_sb[h][0:keys0, 0:w]),
                 dict(start=True, stop=True, skip_group_check=True),
                 reads=("consts", f"P{h}"), writes=(PB[mb],))
            S.op("S", "activation", (lnS_sb[u % 2][:, 0:w], pb[mb][:, 0:w], AF.Ln),
                 reads=(PB[mb],), writes=(f"lnS{u % 2}",))
            S.op("S", "activation", (gbh_sb[u % 2][:, 0:w], lnS_sb[u % 2][:, 0:w], AF.Exp),
                 dict(scale=-1.0, bias=c_b29[:, 0:1]),
                 reads=(f"lnS{u % 2}", "consts"), writes=(f"gbh{u % 2}",))

        def p_out(u):
            qi, h, subs, w, mask = units[u]
            c, po = h // 2, 64 * (h % 2)
            ab = AT_B[u % 2]
            S.op("V", "tensor_tensor",
                 (P_sb[h][:, 0:w], P_sb[h][:, 0:w], gbh_sb[u % 2][:, 0:w], ALU.mult),
                 reads=(f"P{h}", f"gbh{u % 2}"), writes=(f"P{h}",), latch=True)
            col = 0
            for s in subs:
                keys, qn = SUBK[s], SUBQ[s]
                S.op("T", "matmul",
                     (pb[ab][po:po + 64, col:col + qn],
                      vlat_sb[0:keys, E * s + 64 * h:E * s + 64 * (h + 1)],
                      P_sb[h][0:keys, col:col + qn]),
                     dict(start=(s == subs[0]), stop=(s == subs[-1]), skip_group_check=True),
                     reads=("vlat", f"P{h}"), writes=(PB[ab],))
                col += qn
            S.op("V", "tensor_copy",
                 (attn_sb[po:po + 64, TOK * c + QS * subs[0]:TOK * c + QS * subs[0] + w],
                  pb[ab][po:po + 64, 0:w]),
                 reads=(PB[ab],), writes=("attn",))

        for t in range(nU + 2):
            if t < nU:
                p_scores(t)
            if 0 <= t - 1 < nU:
                p_mid(t - 1)
            if 0 <= t - 2 < nU:
                p_out(t - 2)
                if debug and t - 2 <= 1:
                    qi, h, subs, w, mask = units[t - 2]
                    S.op("Y", "dma_start", kw=dict(out=dbg[f"P{h}"].ap(), in_=P_sb[h][:, :]),
                         reads=(f"P{h}",), grp="dbg")

        if debug and stage >= 2:
            S.op("Y", "dma_start", kw=dict(out=dbg["attn"].ap(), in_=attn_sb[:, :]),
                 reads=("attn",), grp="dbg")

        if stage < 3:
            s2.close()
            s12.close()
            attn_stk.close()
            finish()
            return nc

        s2.close()
        s12.close()
        f2 = S.fence()

        x1full_sb = EC(nc.sbuf_tensor("x1full_sb", [128, 16384], BF16))
        s3 = ExitStack()
        wproj_sb = s3.enter_context(nc.sbuf_tensor("wproj_sb", [128, 8 * 1024], BF16))
        xres_sb = [s3.enter_context(nc.sbuf_tensor(f"xres{i}", [128, 4096], F32)) for i in range(2)]
        z_sb = [s3.enter_context(nc.sbuf_tensor(f"z{i}", [128, 4096], F32)) for i in range(2)]
        zb_sb = [s3.enter_context(nc.sbuf_tensor(f"zb{i}", [128, 4096], BF16)) for i in range(2)]
        zsq_sb = [s3.enter_context(nc.sbuf_tensor(f"zsq{i}", [128, 512], BF16)) for i in range(2)]

        S.new_res("wproj", fence=f2)
        for i in range(2):
            for k in ("xres", "z", "zb", "zsq"):
                S.new_res(f"{k}{i}", fence=f2)
        S.new_res("x1full", fence=f2)
        S.new_res("stt3")

        for kc in range(8):
            S.op("G", "dma_start",
                 kw=dict(out=wproj_sb[:, 1024 * kc:1024 * (kc + 1)],
                         in_=d["wprojT"].ap()[128 * kc:128 * (kc + 1), :]),
                 writes=("wproj",), grp="wproj")

        def s3_proj(r):
            sl = r % 2
            S.op("Y", "dma_start",
                 kw=dict(out=xres_sb[sl][:, :].rearrange("p (c n) -> p c n", n=512),
                         in_=d["xT"].ap()[:, 3 + 512 * r:3 + 512 * (r + 1)].rearrange(
                             "(c p) n -> p c n", p=128)),
                 writes=(f"xres{sl}",), grp=f"xres{sl}")
            for fc in range(8):
                bank = (r * 8 + fc) % 2
                for kc in range(8):
                    S.op("T", "matmul",
                         (pb[bank][:, 0:512],
                          wproj_sb[:, 1024 * kc + 128 * fc:1024 * kc + 128 * (fc + 1)],
                          attn_sb[:, TOK * kc + 512 * r:TOK * kc + 512 * (r + 1)]),
                         dict(start=(kc == 0), stop=(kc == 7), skip_group_check=True),
                         reads=("wproj", "attn"), writes=(PB[bank],))
                S.op("V", "scalar_tensor_tensor",
                     (z_sb[sl][:, 512 * fc:512 * (fc + 1)], pb[bank][:, 0:512],
                      c_bproj[:, fc:fc + 1], xres_sb[sl][:, 512 * fc:512 * (fc + 1)],
                      ALU.add, ALU.add),
                     reads=(PB[bank], "consts", f"xres{sl}"), writes=(f"z{sl}",))
                S.op("S", "activation",
                     (zb_sb[sl][:, 512 * fc:512 * (fc + 1)],
                      z_sb[sl][:, 512 * fc:512 * (fc + 1)], AF.Copy),
                     reads=(f"z{sl}",), writes=(f"zb{sl}",))

        def s3_ln(r):
            sl = r % 2
            _ln(S, pb, PB, c_ones, c_onesrow, stt_sb, "stt3",
                z_sb[sl], f"z{sl}", (zb_sb[sl], [512 * c for c in range(8)]), f"zb{sl}",
                zsq_sb, ("zsq0", "zsq1"),
                x1full_sb, "x1full", 512,
                [512 * c for c in range(8)], (2, 3, 4, 5),
                dst_cols=[2048 * c + 512 * r for c in range(8)])

        for r in range(5):
            if r < 4:
                s3_proj(r)
            if r - 1 >= 0:
                s3_ln(r - 1)

        if debug and stage >= 3 and x1_hbm is not None:
            for r in range(4):
                S.op("G", "dma_start",
                     kw=dict(out=x1_hbm.ap()[:, 512 * r:512 * (r + 1)].rearrange(
                                 "(c p) n -> p c n", p=128),
                             in_=x1full_sb[:, :].rearrange("p (c n) -> p c n", n=2048)[
                                 :, :, 512 * r:512 * (r + 1)]),
                     reads=("x1full",))
        if stage < 4:
            s3.close()
            attn_stk.close()
            finish()
            return nc

        s3.close()
        attn_stk.close()
        f3 = S.fence()

        s4 = ExitStack()
        z2_sb = s4.enter_context(nc.sbuf_tensor("z2_sb", [128, 8192], F32))
        wg_sb = [s4.enter_context(nc.sbuf_tensor(f"wg{i}", [128, 4096], BF16)) for i in range(2)]
        wu_sb = [s4.enter_context(nc.sbuf_tensor(f"wu{i}", [128, 4096], BF16)) for i in range(2)]
        wd_sb = [s4.enter_context(nc.sbuf_tensor(f"wd{i}", [128, 2048], BF16)) for i in range(3)]
        gsil_sb = [s4.enter_context(nc.sbuf_tensor(f"gsil{i}", [128, 512], BF16)) for i in range(2)]
        ub_sb = [s4.enter_context(nc.sbuf_tensor(f"ub{i}", [128, 512], BF16)) for i in range(2)]
        a_sb = s4.enter_context(nc.sbuf_tensor("a_sb", [128, 24 * 1024], BF16))
        outh_sb = s4.enter_context(nc.sbuf_tensor("outh_sb", [128, 4096], F32))
        zsqb_sb = [s4.enter_context(nc.sbuf_tensor(f"zsqb{i}", [128, 512], BF16)) for i in range(2)]
        for i in range(2):
            for k in ("wg", "wu", "gsil", "ub", "zsqb"):
                S.new_res(f"{k}{i}", fence=f3)
        for i in range(3):
            S.new_res(f"wd{i}", fence=f3)
        for k in ("z2", "a", "outh"):
            S.new_res(k, fence=f3)

        def s4_gateup(r):
            for h in range(6):
                ws = h % 2
                S.op("G", "dma_start",
                     kw=dict(out=wg_sb[ws][:, :].rearrange("p (c f) -> p c f", f=512),
                             in_=d["wgT"].ap()[:, 512 * h:512 * (h + 1)].rearrange(
                                 "(c p) f -> p c f", p=128)),
                     writes=(f"wg{ws}",), grp=f"wg{ws}")
                S.op("G", "dma_start",
                     kw=dict(out=wu_sb[ws][:, :].rearrange("p (c f) -> p c f", f=512),
                             in_=d["wuT"].ap()[:, 512 * h:512 * (h + 1)].rearrange(
                                 "(c p) f -> p c f", p=128)),
                     writes=(f"wu{ws}",), grp=f"wu{ws}")
                for hf in range(4):
                    for half in range(2):
                        gbank = (hf * 2 + half) % 2
                        ubank = 2 + gbank
                        for kc in range(8):
                            rhs = x1full_sb[:, 2048 * kc + 1024 * r + 512 * half:
                                            2048 * kc + 1024 * r + 512 * (half + 1)]
                            S.op("T", "matmul",
                                 (pb[gbank][:, 0:512],
                                  wg_sb[ws][:, 512 * kc + 128 * hf:512 * kc + 128 * (hf + 1)],
                                  rhs),
                                 dict(start=(kc == 0), stop=(kc == 7), skip_group_check=True),
                                 reads=(f"wg{ws}", "x1full"), writes=(PB[gbank],))
                            S.op("T", "matmul",
                                 (pb[ubank][:, 0:512],
                                  wu_sb[ws][:, 512 * kc + 128 * hf:512 * kc + 128 * (hf + 1)],
                                  rhs),
                                 dict(start=(kc == 0), stop=(kc == 7), skip_group_check=True),
                                 reads=(f"wu{ws}", "x1full"), writes=(PB[ubank],))
                        gs = (hf * 2 + half) % 2
                        S.op("S", "activation",
                             (gsil_sb[gs][:, 0:512], pb[gbank][:, 0:512], AF.Silu),
                             dict(scale=INV127),
                             reads=(PB[gbank],), writes=(f"gsil{gs}",))
                        S.op("V", "tensor_scalar",
                             (ub_sb[gs][:, 0:512], pb[ubank][:, 0:512], INV127SQ, None, ALU.mult),
                             reads=(PB[ubank],), writes=(f"ub{gs}",))
                        ahc = 4 * h + hf
                        S.op("V", "tensor_tensor",
                             (a_sb[:, 1024 * ahc + 512 * half:1024 * ahc + 512 * (half + 1)],
                              gsil_sb[gs][:, 0:512], ub_sb[gs][:, 0:512], ALU.mult),
                             reads=(f"gsil{gs}", f"ub{gs}"), writes=("a",))

        def s4_down(r):
            if debug:
                S.op("Y", "dma_start", kw=dict(out=dbg["a4"].ap()[r], in_=a_sb[:, :]),
                     reads=("a",), grp="dbg")
                if r == 0:
                    S.op("Y", "dma_start", kw=dict(out=dbg["x1full"].ap(), in_=x1full_sb[:, :]),
                         reads=("x1full",), grp="dbg")
            for fcg in range(2):
                for hcp in range(12):
                    ws = hcp % 3
                    S.op("G", "dma_start",
                         kw=dict(out=wd_sb[ws][:, :].rearrange("p (c e) -> p c e", e=1024),
                                 in_=d["wdT"].ap()[256 * hcp:256 * (hcp + 1), :].rearrange(
                                     "(c p) e -> p c e", p=128)),
                         writes=(f"wd{ws}",), grp=f"wd{ws}")
                    for hi in range(2):
                        hc = 2 * hcp + hi
                        for fi in range(4):
                            fc = 4 * fcg + fi
                            for half in range(2):
                                bank = fi * 2 + half
                                S.op("T", "matmul",
                                     (pb[bank][:, 0:512],
                                      wd_sb[ws][:, 1024 * hi + 128 * fc:1024 * hi + 128 * (fc + 1)],
                                      a_sb[:, 1024 * hc + 512 * half:1024 * hc + 512 * (half + 1)]),
                                     dict(start=(hc == 0), stop=(hc == 23), skip_group_check=True),
                                     reads=(f"wd{ws}", "a"), writes=(PB[bank],))
                for fi in range(4):
                    fc = 4 * fcg + fi
                    for half in range(2):
                        bank = fi * 2 + half
                        cc = 1024 * fc + 512 * half
                        x1c = 2048 * fc + 1024 * r + 512 * half
                        S.op("V", "tensor_tensor",
                             (z2_sb[:, cc:cc + 512], pb[bank][:, 0:512],
                              x1full_sb[:, x1c:x1c + 512], ALU.add),
                             reads=(PB[bank], "x1full"), writes=("z2",))

        def s4_ln2(r):
            if debug:
                S.op("Y", "dma_start", kw=dict(out=dbg["z2"].ap()[r], in_=z2_sb[:, :]),
                     reads=("z2",), grp="dbg")
            for half in range(2):
                cols = [1024 * c + 512 * half for c in range(8)]
                for c in range(8):
                    S.op("S", "activation",
                         (a_sb[:, cols[c]:cols[c] + 512],
                          z2_sb[:, cols[c]:cols[c] + 512], AF.Copy),
                         reads=("z2",), writes=("a",))
                _ln(S, pb, PB, c_ones, c_onesrow, stt_sb, "stt3",
                    z2_sb, "z2", (a_sb, cols), "a",
                    zsqb_sb, ("zsqb0", "zsqb1"),
                    outh_sb, "outh", 512, cols, (4, 5, 6, 7),
                    dst_cols=[512 * c for c in range(8)])
                S.op("Y", "dma_start",
                     kw=dict(out=y.ap()[:, 1024 * r + 512 * half:1024 * r + 512 * (half + 1)]
                             .rearrange("(c p) n -> p c n", p=128),
                             in_=outh_sb[:, :].rearrange("p (c n) -> p c n", n=512)),
                     reads=("outh",), grp="yst")

        s4_gateup(0)
        s4_down(0)
        s4_ln2(0)
        s4_gateup(1)
        s4_down(1)
        s4_ln2(1)

        s4.close()
        finish()
    return nc



def _window_mask_blocks():
    std = np.zeros((128, 122), dtype=np.float32)
    for t in range(122):
        std[t:t + 7, t] = 1.0
    w0 = std.copy()
    w0[0:3, :] = 0.0
    w0[0, 0], w0[0, 1], w0[0, 2] = 3.0, 2.0, 1.0
    wl = np.zeros((128, 96), dtype=np.float32)
    for t in range(96):
        wl[t:t + 7, t] = 1.0
    wl[102:, :] = 0.0
    wl7 = wl.copy()
    wl7[99:102, :] = 0.0
    wl7[99, 93], wl7[99, 94], wl7[99, 95] = 1.0, 2.0, 3.0
    return std, w0, wl, wl7


_CACHE = {}


def kernel(L0_attn_x, L0_attn_W_qkv, L0_attn_b_qkv, L0_attn_W_lat, L0_attn_b_lat,
           L0_attn_W_proj, L0_attn_b_proj, L0_ffn_w_gate, L0_ffn_w_up, L0_ffn_w_down):
    bf = ml_dtypes.bfloat16
    x = np.ascontiguousarray(np.asarray(L0_attn_x, dtype=np.float32)[0])
    Wqkv = np.asarray(L0_attn_W_qkv, dtype=np.float32)
    bqkv = np.asarray(L0_attn_b_qkv, dtype=np.float32)
    Wlat = np.asarray(L0_attn_W_lat, dtype=np.float32)
    blat = np.asarray(L0_attn_b_lat, dtype=np.float32)
    Wproj = np.asarray(L0_attn_W_proj, dtype=np.float32)
    bproj = np.asarray(L0_attn_b_proj, dtype=np.float32)

    wq_lay = np.ascontiguousarray(Wqkv.T)
    wqkv_r = np.ascontiguousarray(wq_lay.reshape(8, 128, 24, 128).transpose(2, 0, 1, 3))
    wprojT = np.ascontiguousarray(Wproj.T)
    wlat2 = np.ascontiguousarray(np.vstack([Wlat.T, Wlat.T]))
    wgT = np.ascontiguousarray(np.asarray(L0_ffn_w_gate, dtype=np.int32).T)
    wuT = np.ascontiguousarray(np.asarray(L0_ffn_w_up, dtype=np.int32).T)
    wdT = np.ascontiguousarray(np.asarray(L0_ffn_w_down, dtype=np.int32).T)
    bqkv_col = np.ascontiguousarray(bqkv.reshape(24, 128).T)
    blat_col = np.ascontiguousarray(np.concatenate([blat, blat])[:, None])
    blat_b512 = np.ascontiguousarray(np.tile(blat, (128, 8)))
    bproj_col = np.ascontiguousarray(bproj.reshape(8, 128).T)
    ones_col = np.ones((128, 1), dtype=bf)
    ones_row_f32 = np.ones((128, 128), dtype=np.float32)
    ones_sel = np.zeros((128, 256), dtype=np.float32)
    for h in range(16):
        ones_sel[:, 16 * h + h] = 1.0
    ind = np.zeros((128, 128), dtype=np.float32)
    for c in range(8):
        ind[0:64, 16 * c + 2 * c] = 1.0
        ind[64:128, 16 * c + 2 * c + 1] = 1.0
    selbc = np.zeros((128, 16 * 128), dtype=np.float32)
    for h in range(16):
        selbc[h, 128 * h:128 * (h + 1)] = 1.0

    std, w0blk, wl_std, wl7 = _window_mask_blocks()
    mqs = np.zeros((128, 488), dtype=np.float32)
    for j in range(4):
        mqs[:, 122 * j:122 * (j + 1)] = std

    key = ("prog", STAGE, DEBUG)
    if key not in _CACHE:
        _CACHE[key] = build_program(STAGE, DEBUG)
    nc = _CACHE[key]

    in_maps = []
    for c in range(NCORE):
        base = 2048 * c
        rows = np.arange(base - 3, base + 2051)
        if c == 0:
            rows = rows.copy()
            rows[0] = SEQ - 1
            rows[1] = -1
            rows[2] = -1
        if c == NCORE - 1:
            rows = rows.copy()
            rows[2051] = 0
            rows[2052] = -1
            rows[2053] = -1
        xc = np.zeros((A, E), dtype=np.float32)
        valid = rows >= 0
        xc[valid] = x[rows[valid]]
        xT = np.ascontiguousarray(xc.T)

        mq0 = mqs.copy()
        mq0[:, 0:122] = w0blk if c == 0 else std
        mlast = wl7 if c == NCORE - 1 else wl_std

        in_maps.append(dict(
            xT=xT, wqkv_r=wqkv_r, wprojT=wprojT, wlat2=wlat2,
            wgT=wgT, wuT=wuT, wdT=wdT,
            bqkv_col=bqkv_col, blat_col=blat_col, blat_b512=blat_b512,
            bproj_col=bproj_col,
            mq0=mq0.astype(bf), mqs=mqs.astype(bf), mlast=mlast.astype(bf),
            ones_col=ones_col, ones_row_f32=ones_row_f32,
            ones_sel=ones_sel.astype(bf), ones128=np.full((128, 128), 2.0**29, dtype=bf),
            b29=np.full((128, 1), 29.0 * np.log(2.0), dtype=np.float32),
            ones_col_f32=np.ones((128, 1), dtype=np.float32),
            selbc=selbc.astype(bf), ind=ind.astype(bf),
        ))

    res = run_bass_kernel_spmd(nc, in_maps, core_ids=list(range(NCORE)))
    out = np.empty((1, SEQ, E), dtype=np.float32)
    for c in range(NCORE):
        out[0, 2048 * c:2048 * (c + 1), :] = np.asarray(res.results[c]["y"]).T
    if DEBUG:
        kernel._last = res
    return out


# version 13
# speedup vs baseline: 1.0448x; 1.0139x over previous
import numpy as np
import ml_dtypes
from contextlib import ExitStack

import concourse.bass as bass
import concourse.mybir as mybir
from concourse.bass_utils import run_bass_kernel_spmd

F32 = mybir.dt.float32
BF16 = mybir.dt.bfloat16
I32 = mybir.dt.int32
AF = mybir.ActivationFunctionType
ALU = mybir.AluOpType
AX = mybir.AxisListType

NCORE = 8
SEQ = 16384
E = 1024
NH = 16
DH = 64
HID = 3072
TOK = 2048
A = TOK + 6
QS = 122
NSUB = 17
SUBQ = [122] * 16 + [96]
SUBK = [128] * 16 + [102]
QUADS = [(0, 4), (4, 8), (8, 12), (12, 16), (16, 17)]
R5 = [(0, 512), (512, 1024), (1024, 1536), (1536, 2048), (2048, 2054)]
INV127 = 1.0 / 127.0
INV127SQ = INV127 * INV127

STAGE = 4
DEBUG = False
DBGQ = 0

ENGS = ("T", "V", "S", "Y", "G")
BLOCK_ATTR = {"T": "tensor", "V": "vector", "S": "scalar", "Y": "sync", "G": "gpsimd"}
DMA_ENGS = ("Y", "G")


class _Op:
    __slots__ = ("eng", "name", "args", "kw", "deps", "gdeps", "inc", "cum", "idx", "fragile",
                 "grp")


class Sched:
    def __init__(self):
        self.ops = {e: [] for e in ENGS}
        self.res = {}
        self.gcnt = {}
        self.glast = {}

    def new_res(self, key, fence=None):
        self.res[key] = {"w": None, "r": [], "f": list(fence or [])}

    def fence(self):
        return [(e, len(self.ops[e]) - 1) for e in ENGS if self.ops[e]]

    def op(self, eng, name, args=(), kw=None, reads=(), writes=(), extra_deps=(), latch=False,
           fragile=False, grp=None):
        idx = len(self.ops[eng])
        deps = {}
        gdeps = {}

        def add(d):
            de, di = d
            if di < 0 or (de == eng and di >= idx):
                return
            t = self.ops[de][di]
            if de in DMA_ENGS and t.name == "dma_start":
                g = t.grp
                c = self.gcnt[g]
                if gdeps.get(g, 0) < c:
                    gdeps[g] = c
                return
            if de in DMA_ENGS:
                return
            if de == eng and not latch and not (idx - di <= 2):
                return
            if deps.get(de, -1) < di:
                deps[de] = di

        for k in reads:
            r = self.res[k]
            for dd in r["f"]:
                add(dd)
            r["f"] = []
            if r["w"]:
                add(r["w"])
            r["r"].append((eng, idx))
        for k in writes:
            r = self.res[k]
            for dd in r["f"]:
                add(dd)
            r["f"] = []
            if r["w"]:
                add(r["w"])
            for dd in r["r"]:
                add(dd)
            r["r"] = []
            r["w"] = (eng, idx)
        for dd in extra_deps:
            add(dd)
        o = _Op()
        o.eng, o.name, o.args, o.kw = eng, name, args, (kw or {})
        o.deps, o.gdeps, o.inc, o.idx = deps, gdeps, False, idx
        o.fragile = fragile
        o.grp = None
        if eng in DMA_ENGS and name == "dma_start":
            assert grp is not None, "dma_start needs a semaphore group"
            o.grp = grp
            self.gcnt[grp] = self.gcnt.get(grp, 0) + 1
            self.glast[grp] = (eng, idx)
        for de, di in deps.items():
            self.ops[de][di].inc = True
        self.ops[eng].append(o)
        return (eng, idx)

    def finalize(self):
        for e in ENGS:
            c = 0
            for o in self.ops[e]:
                if o.grp is None and o.inc:
                    c += 1
                o.cum = c

    def emit(self, block, sems):
        self.finalize()

        def runner(e):
            def run(engobj):
                mw = {}
                for o in self.ops[e]:
                    for de in sorted(o.deps):
                        di = o.deps[de]
                        val = self.ops[de][di].cum
                        assert val > 0, f"{e} op {o.name} depends on un-inc'd {de}[{di}]"
                        if mw.get(de, -1) >= val:
                            continue
                        mw[de] = val
                        engobj.wait_ge(sems[de], val)
                    for g in sorted(o.gdeps):
                        val = 16 * o.gdeps[g]
                        if mw.get(g, -1) >= val:
                            continue
                        mw[g] = val
                        engobj.wait_ge(sems[g], val)
                    if o.name == "wait_only":
                        continue
                    inst = getattr(engobj, o.name)(*o.args, **o.kw)
                    if o.grp is not None:
                        inst.then_inc(sems[o.grp], 16)
                    elif o.inc:
                        inst.then_inc(sems[e], 1)
            return run

        for e in ENGS:
            getattr(block, BLOCK_ATTR[e])(runner(e))


def _ln(S, pb, PB, c_ones, c_onesrow, stt, stt_key, src_sb, src_key, zb, zb_key,
        zsq, zsq_keys, dst_sb, dst_key, n, cols, banks, dst_cols=None):
    if dst_cols is None:
        dst_cols = cols
    """LayerNorm over the feature dim (8 chunks x 128 partitions) for n tokens.

    zb must hold a bf16 copy of src (stats operand). Overwrites src in place
    (src - mean); writes the normalized fp32 result into dst.
    """
    sb1, sb2, bb1, bb2 = banks
    m_sb = stt[0:1, 0:n]
    e2_sb = stt[0:1, 512:512 + n]
    m2_sb = stt[0:1, 1024:1024 + n]
    var_sb = stt[0:1, 1536:1536 + n]
    lnv_sb = stt[0:1, 1024:1024 + n]
    rstd_sb = stt[0:1, 2048:2048 + n]
    for c in range(8):
        S.op("T", "matmul", (pb[sb1][0:1, 0:n], c_ones[:, 0:1], zb[0][:, zb[1][c]:zb[1][c] + n]),
             dict(start=(c == 0), stop=(c == 7), skip_group_check=True),
             reads=("consts", zb_key), writes=(PB[sb1],))
    for c in range(8):
        zs = zsq[c % 2]
        S.op("S", "activation", (zs[:, 0:n], zb[0][:, zb[1][c]:zb[1][c] + n], AF.Square),
             reads=(zb_key,), writes=(zsq_keys[c % 2],))
        S.op("T", "matmul", (pb[sb2][0:1, 0:n], c_ones[:, 0:1], zs[:, 0:n]),
             dict(start=(c == 0), stop=(c == 7), skip_group_check=True),
             reads=("consts", zsq_keys[c % 2]), writes=(PB[sb2],))
    S.op("V", "tensor_scalar", (m_sb, pb[sb1][0:1, 0:n], 1.0 / E, None, ALU.mult),
         reads=(PB[sb1],), writes=(stt_key,))
    S.op("V", "tensor_scalar", (e2_sb, pb[sb2][0:1, 0:n], 1.0 / E, None, ALU.mult),
         reads=(PB[sb2],), writes=(stt_key,))
    S.op("V", "tensor_tensor", (m2_sb, m_sb, m_sb, ALU.mult), reads=(stt_key,), writes=(stt_key,))
    S.op("V", "tensor_tensor", (var_sb, e2_sb, m2_sb, ALU.subtract), reads=(stt_key,), writes=(stt_key,))
    S.op("V", "tensor_scalar", (var_sb, var_sb, 1e-5, None, ALU.add),
         reads=(stt_key,), writes=(stt_key,))
    S.op("S", "activation", (lnv_sb, var_sb, AF.Ln),
         reads=(stt_key,), writes=(stt_key,))
    S.op("S", "activation", (rstd_sb, lnv_sb, AF.Exp), dict(scale=-0.5),
         reads=(stt_key,), writes=(stt_key,))
    S.op("T", "matmul", (pb[bb1][:, 0:n], c_onesrow[0:1, :], m_sb),
         dict(start=True, stop=True, skip_group_check=True),
         reads=("consts", stt_key), writes=(PB[bb1],))
    S.op("T", "matmul", (pb[bb2][:, 0:n], c_onesrow[0:1, :], rstd_sb),
         dict(start=True, stop=True, skip_group_check=True),
         reads=("consts", stt_key), writes=(PB[bb2],))
    S.op("T", "matmul", (pb[sb1][0:1, 0:1], c_ones[0:1, 0:1], c_ones[0:1, 0:1]),
         dict(start=True, stop=True, skip_group_check=True),
         reads=("consts",), writes=(PB[bb1], PB[bb2], PB[sb1]))
    prev = None
    for c in range(8):
        cc = cols[c]
        S.op("V", "tensor_tensor",
             (src_sb[:, cc:cc + n], src_sb[:, cc:cc + n], pb[bb1][:, 0:n], ALU.subtract),
             reads=(src_key, PB[bb1]), writes=(src_key,))
        if prev is not None:
            S.op("V", "tensor_tensor",
                 (dst_sb[:, prev[1]:prev[1] + n], src_sb[:, prev[0]:prev[0] + n],
                  pb[bb2][:, 0:n], ALU.mult),
                 reads=(src_key, PB[bb2]), writes=(dst_key,))
        prev = (cc, dst_cols[c])
    S.op("V", "tensor_tensor",
         (dst_sb[:, prev[1]:prev[1] + n], src_sb[:, prev[0]:prev[0] + n],
          pb[bb2][:, 0:n], ALU.mult),
         reads=(src_key, PB[bb2]), writes=(dst_key,))


def build_program(stage=None, debug=None):
    stage = STAGE if stage is None else stage
    debug = DEBUG if debug is None else debug
    nc = bass.Bass("TRN2", target_bir_lowering=False)

    d = {}
    def din(name, shape, dt):
        d[name] = nc.dram_tensor(name, list(shape), dt, kind="ExternalInput")

    din("xT", (E, A), F32)
    din("wqkv_r", (24, 8, 128, 128), F32)
    din("wprojT", (E, E), F32)
    din("wlat2", (128, DH), F32)
    din("wgT", (E, HID), I32)
    din("wuT", (E, HID), I32)
    din("wdT", (HID, E), I32)
    din("bqkv_col", (128, 24), F32)
    din("blat_col", (128, 1), F32)
    din("blat_b512", (128, 512), F32)
    din("bproj_col", (128, 8), F32)
    din("mq0", (128, 488), BF16)
    din("mqs", (128, 488), BF16)
    din("mlast", (128, 96), BF16)
    din("ones_col", (128, 1), BF16)
    din("ones_row_f32", (128, 128), F32)
    din("ones_sel", (128, 256), BF16)
    din("ones128", (128, 128), BF16)
    din("selbc", (128, 16 * 128), BF16)
    din("b29", (128, 1), F32)
    din("ones_col_f32", (128, 1), F32)
    din("ind", (128, 128), BF16)

    x1_hbm = nc.dram_tensor("x1_dbg", [E, TOK], F32, kind="ExternalOutput") \
        if (debug and stage >= 3) else None
    y = nc.dram_tensor("y", [E, TOK], F32, kind="ExternalOutput")
    dbg = {}
    if debug:
        if stage >= 1:
            dbg["q"] = nc.dram_tensor("dbg_q", [128, 8 * A], BF16, kind="ExternalOutput")
            dbg["klat"] = nc.dram_tensor("dbg_klat", [128, 8 * A], BF16, kind="ExternalOutput")
            dbg["vlat"] = nc.dram_tensor("dbg_vlat", [128, NSUB * E], BF16, kind="ExternalOutput")
            dbg["negD"] = nc.dram_tensor("dbg_negD", [128, TOK], BF16, kind="ExternalOutput")
        if stage >= 2:
            dbg["attn"] = nc.dram_tensor("dbg_attn", [128, 8 * TOK], BF16, kind="ExternalOutput")
        if stage >= 4:
            dbg["x1full"] = nc.dram_tensor("dbg_x1full", [128, 16384], BF16, kind="ExternalOutput")
            dbg["z2"] = nc.dram_tensor("dbg_z2", [2, 128, 8192], F32, kind="ExternalOutput")
            dbg["a4"] = nc.dram_tensor("dbg_a4", [2, 128, 24576], BF16, kind="ExternalOutput")
            dbg["P0"] = nc.dram_tensor("dbg_P0", [128, 488], BF16, kind="ExternalOutput")
            dbg["P1"] = nc.dram_tensor("dbg_P1", [128, 488], BF16, kind="ExternalOutput")


    S = Sched()
    ctx = ExitStack()
    EC = ctx.enter_context

    with ctx:
        pb = [EC(nc.psum_tensor(f"pb{i}", [128, 512], F32)) for i in range(8)]
        PB = [f"pb{i}" for i in range(8)]
        for i in range(8):
            S.new_res(PB[i])

        sems = {e: EC(nc.semaphore(f"sem{e}")) for e in ENGS}
        block = EC(nc.Block())

        def finish():
            S.op("Y", "wait_only", extra_deps=list(S.glast.values()))
            for g in S.gcnt:
                sems[g] = EC(nc.semaphore(f"sg_{g}"))
            with nc.allow_low_precision("bf16 matmul pipeline with fp32 accumulation by design"):
                S.emit(block, sems)

        c_bqkv = EC(nc.sbuf_tensor("c_bqkv", [128, 24], F32))
        c_blat = EC(nc.sbuf_tensor("c_blat", [128, 1], F32))
        c_blat512 = EC(nc.sbuf_tensor("c_blat512", [128, 512], F32))
        c_bproj = EC(nc.sbuf_tensor("c_bproj", [128, 8], F32))
        c_mq0 = EC(nc.sbuf_tensor("c_mq0", [128, 488], BF16))
        c_mqs = EC(nc.sbuf_tensor("c_mqs", [128, 488], BF16))
        c_mlast = EC(nc.sbuf_tensor("c_mlast", [128, 96], BF16))
        c_ones = EC(nc.sbuf_tensor("c_ones", [128, 1], BF16))
        c_onesrow = EC(nc.sbuf_tensor("c_onesrow", [128, 128], F32))
        c_osel = EC(nc.sbuf_tensor("c_osel", [128, 256], BF16))
        c_ones128 = EC(nc.sbuf_tensor("c_ones128", [128, 128], BF16))
        c_selbc = EC(nc.sbuf_tensor("c_selbc", [128, 16 * 128], BF16))
        c_b29 = EC(nc.sbuf_tensor("c_b29", [128, 1], F32))
        c_onesf = EC(nc.sbuf_tensor("c_onesf", [128, 1], F32))
        stt_sb = EC(nc.sbuf_tensor("stt_sb", [1, 5 * 512], F32))
        c_ind = EC(nc.sbuf_tensor("c_ind", [128, 128], BF16))
        c_wlat = EC(nc.sbuf_tensor("c_wlat", [128, DH], BF16))
        S.new_res("consts")
        for name, t in [("bqkv_col", c_bqkv), ("blat_col", c_blat), ("blat_b512", c_blat512),
                        ("bproj_col", c_bproj), ("mq0", c_mq0), ("mqs", c_mqs),
                        ("mlast", c_mlast), ("ones_col", c_ones), ("ones_row_f32", c_onesrow),
                        ("ones_sel", c_osel), ("ones128", c_ones128), ("selbc", c_selbc),
                        ("b29", c_b29), ("ones_col_f32", c_onesf), ("ind", c_ind)]:
            S.op("Y", "dma_start", kw=dict(out=t[:, :], in_=d[name].ap()), writes=("consts",),
                 grp="cY")
        S.op("G", "dma_start", kw=dict(out=c_wlat[:, :], in_=d["wlat2"].ap()), writes=("consts",),
             grp="cG")

        def hot(k=1):
            for _ in range(k):
                S.op("T", "matmul", (pb[7][:, 0:512], c_selbc[:, 0:128], c_selbc[:, 0:512]),
                     dict(start=True, stop=True, skip_group_check=True), reads=("consts",))

        hot(192)

        s12 = ExitStack()
        q_sb = s12.enter_context(nc.sbuf_tensor("q_sb", [128, 8 * A], BF16))
        klat_sb = s12.enter_context(nc.sbuf_tensor("klat_sb", [128, 8 * A], BF16))
        vlat_sb = s12.enter_context(nc.sbuf_tensor("vlat_sb", [128, NSUB * E], BF16))
        qn2_sb = s12.enter_context(nc.sbuf_tensor("qn2_sb", [16, A], F32))
        kn2_sb = s12.enter_context(nc.sbuf_tensor("kn2_sb", [16, A], F32))
        wka_sb = s12.enter_context(nc.sbuf_tensor("wka_sb", [16, A], F32))
        negD_sb = s12.enter_context(nc.sbuf_tensor("negD_sb", [16, TOK], BF16))
        for k in ("q", "klat", "vlat", "qn2", "kn2", "wka", "negD"):
            S.new_res(k)

        s1 = ExitStack()
        xb_sb = [s1.enter_context(nc.sbuf_tensor(f"xb{c}", [128, A], BF16)) for c in range(8)]
        wq_sb = [s1.enter_context(nc.sbuf_tensor(f"wq{i}", [128, 1024], BF16)) for i in range(2)]
        kfm_sb = [s1.enter_context(nc.sbuf_tensor(f"kfm{i}", [128, A], BF16)) for i in range(2)]
        vfm_sb = [s1.enter_context(nc.sbuf_tensor(f"vfm{i}", [128, A], BF16)) for i in range(2)]
        sq_sb = [s1.enter_context(nc.sbuf_tensor(f"sq{i}", [128, 512], BF16)) for i in range(2)]
        for c in range(8):
            S.new_res(f"xb{c}")
        for i in range(2):
            for k in ("wq", "kfm", "vfm", "sq"):
                S.new_res(f"{k}{i}")


        def wq_load(fc):
            slot = fc % 2
            src = d["wqkv_r"].ap()[fc:fc + 1].rearrange("o c p f -> p (o c) f")
            S.op("G", "dma_start",
                 kw=dict(out=wq_sb[slot][:, :].rearrange("p (c f) -> p c f", f=128), in_=src),
                 writes=(f"wq{slot}",), grp=f"wq{slot}")

        wq_load(0)
        wq_load(1)
        for c in range(8):
            S.op("G", "dma_start",
                 kw=dict(out=xb_sb[c][:, :], in_=d["xT"].ap()[128 * c:128 * (c + 1), :]),
                 writes=(f"xb{c}",), grp="xb")

        KE_B, KO_B, VE_B, VO_B, NRM_B = 2, 3, 4, 5, 6

        def norm_pass(src_sb, src_key, dst_sb, dst_key):
            jobsn = [(ri, c) for ri in range(5) for c in range(8)]

            def n_sq(i):
                ri, c = jobsn[i]
                r0, r1 = R5[ri]
                S.op("S", "activation",
                     (sq_sb[i % 2][:, 0:r1 - r0], src_sb[:, A * c + r0:A * c + r1], AF.Square),
                     reads=(src_key,), writes=(f"sq{i % 2}",))

            def n_mm(i):
                ri, c = jobsn[i]
                r0, r1 = R5[ri]
                n = r1 - r0
                S.op("T", "matmul",
                     (pb[NRM_B][0:16, 0:n], c_ind[:, 16 * c:16 * (c + 1)], sq_sb[i % 2][:, 0:n]),
                     dict(start=(c == 0), stop=(c == 7), skip_group_check=True),
                     reads=("consts", f"sq{i % 2}"), writes=(PB[NRM_B],))
                if c == 7:
                    S.op("V", "tensor_copy", (dst_sb[0:16, r0:r1], pb[NRM_B][0:16, 0:n]),
                         reads=(PB[NRM_B],), writes=(dst_key,))

            for i in range(41):
                if i < 40:
                    n_sq(i)
                if i - 1 >= 0:
                    n_mm(i - 1)

        jobs1 = [(fc, ri) for fc in range(24) for ri in range(5)]

        def s1_group(j):
            fc, ri = jobs1[j]
            slot = fc % 2
            r0, r1 = R5[ri]
            n = r1 - r0
            bank = j % 2
            for kc in range(8):
                S.op("T", "matmul",
                     (pb[bank][:, 0:n], wq_sb[slot][:, 128 * kc:128 * (kc + 1)],
                      xb_sb[kc][:, r0:r1]),
                     dict(start=(kc == 0), stop=(kc == 7), skip_group_check=True),
                     reads=(f"wq{slot}", f"xb{kc}"), writes=(PB[bank],))
            if ri == 4 and fc + 2 < 24:
                wq_load(fc + 2)

        def s1_evac(j):
            fc, ri = jobs1[j]
            slot = fc % 2
            r0, r1 = R5[ri]
            n = r1 - r0
            bank = j % 2
            kind = 0 if fc < 8 else (1 if fc < 16 else 2)
            c = fc % 8
            if kind == 0:
                dst_sb, dst_key, dcol = q_sb, "q", A * c
            elif kind == 1:
                dst_sb, dst_key, dcol = kfm_sb[slot], f"kfm{slot}", 0
            else:
                dst_sb, dst_key, dcol = vfm_sb[slot], f"vfm{slot}", 0
            S.op("S", "activation",
                 (dst_sb[:, dcol + r0:dcol + r1], pb[bank][:, 0:n], AF.Identity),
                 dict(bias=c_bqkv[:, fc:fc + 1]),
                 reads=(PB[bank], "consts"), writes=(dst_key,))

        def s1_follow(j):
            fc, ri = jobs1[j]
            slot = fc % 2
            r0, r1 = R5[ri]
            n = r1 - r0
            kind = 0 if fc < 8 else (1 if fc < 16 else 2)
            c = fc % 8
            if kind == 1:
                S.op("T", "matmul",
                     (pb[KE_B][0:64, 0:n], c_wlat[0:64, :], kfm_sb[slot][0:64, r0:r1]),
                     dict(start=True, stop=True, skip_group_check=True),
                     reads=("consts", f"kfm{slot}"), writes=(PB[KE_B],))
                S.op("T", "matmul",
                     (pb[KO_B][64:128, 0:n], c_wlat[64:128, :], kfm_sb[slot][64:128, r0:r1]),
                     dict(start=True, stop=True, skip_group_check=True),
                     reads=("consts", f"kfm{slot}"), writes=(PB[KO_B],))
                S.op("S", "activation",
                     (klat_sb[0:64, A * c + r0:A * c + r1], pb[KE_B][0:64, 0:n], AF.Identity),
                     dict(bias=c_blat[0:64, 0:1]),
                     reads=(PB[KE_B], "consts"), writes=("klat",))
                S.op("S", "activation",
                     (klat_sb[64:128, A * c + r0:A * c + r1], pb[KO_B][64:128, 0:n], AF.Identity),
                     dict(bias=c_blat[64:128, 0:1]),
                     reads=(PB[KO_B], "consts"), writes=("klat",))
            if kind == 2 and ri == 4:
                for sg0, sg1 in ((0, 8), (8, 16), (16, 17)):
                    glen = sg1 - sg0
                    for par, bank in ((0, VE_B), (1, VO_B)):
                        p0 = 64 * par
                        for jj, s in enumerate(range(sg0, sg1)):
                            wlen = 128 if s < 16 else 102
                            S.op("T", "matmul",
                                 (pb[bank][0:wlen, 64 * jj:64 * (jj + 1)],
                                  vfm_sb[slot][p0:p0 + 64, QS * s:QS * s + wlen],
                                  c_wlat[p0:p0 + 64, :]),
                                 dict(start=(jj == 0), stop=(jj == glen - 1), skip_group_check=True),
                                 reads=(f"vfm{slot}", "consts"), writes=(PB[bank],))
                        dst = vlat_sb[:, :].rearrange("p (s e) -> p s e", e=E)[
                            :, sg0:sg1, 128 * c + 64 * par:128 * c + 64 * par + 64]
                        S.op("V", "tensor_tensor",
                             (dst,
                              pb[bank][:, 0:64 * glen].rearrange("p (s e) -> p s e", e=64),
                              c_blat512[:, 0:64 * glen].rearrange("p (s e) -> p s e", e=64),
                              ALU.add),
                             reads=(PB[bank], "consts"), writes=("vlat",))
            if fc == 7 and ri == 4:
                norm_pass(q_sb, "q", qn2_sb, "qn2")
            if fc == 15 and ri == 4:
                norm_pass(klat_sb, "klat", kn2_sb, "kn2")

        for j in range(122):
            if j < 120:
                s1_group(j)
            if 0 <= j - 1 < 120:
                s1_evac(j - 1)
            if 0 <= j - 2 < 120:
                s1_follow(j - 2)

        S.op("V", "tensor_tensor", (wka_sb[0:16, 0:A - 1], kn2_sb[0:16, 0:A - 1],
                                    kn2_sb[0:16, 1:A], ALU.max),
             reads=("kn2",), writes=("wka",))
        S.op("V", "tensor_tensor", (kn2_sb[0:16, 0:A - 3], wka_sb[0:16, 0:A - 3],
                                    wka_sb[0:16, 2:A - 1], ALU.max),
             reads=("wka", "kn2"), writes=("kn2",))
        S.op("V", "tensor_tensor", (wka_sb[0:16, 0:TOK], kn2_sb[0:16, 0:TOK],
                                    kn2_sb[0:16, 3:TOK + 3], ALU.max),
             reads=("kn2",), writes=("wka",))
        HT = TOK // 2
        for hh in range(2):
            o = HT * hh
            S.op("V", "tensor_scalar", (wka_sb[0:16, o:o + HT], wka_sb[0:16, o:o + HT],
                                        1e-30, None, ALU.add),
                 reads=("wka",), writes=("wka",))
            S.op("V", "tensor_scalar", (qn2_sb[0:16, 3 + o:3 + o + HT],
                                        qn2_sb[0:16, 3 + o:3 + o + HT], 1e-30, None, ALU.add),
                 reads=("qn2",), writes=("qn2",))
        for hh in range(2):
            o = HT * hh
            S.op("S", "activation", (kn2_sb[0:16, o:o + HT], wka_sb[0:16, o:o + HT], AF.Ln),
                 reads=("wka",), writes=("kn2",))
            S.op("S", "activation", (wka_sb[0:16, o:o + HT], qn2_sb[0:16, 3 + o:3 + o + HT],
                                     AF.Ln),
                 reads=("qn2",), writes=("wka",))
        for hh in range(2):
            o = HT * hh
            S.op("V", "tensor_tensor", (qn2_sb[0:16, o:o + HT], wka_sb[0:16, o:o + HT],
                                        kn2_sb[0:16, o:o + HT], ALU.add),
                 reads=("wka", "kn2"), writes=("qn2",))
        for hh in range(2):
            o = HT * hh
            S.op("S", "activation", (wka_sb[0:16, o:o + HT], qn2_sb[0:16, o:o + HT], AF.Exp),
                 dict(scale=0.5), reads=("qn2",), writes=("wka",))
        for hh in range(2):
            o = HT * hh
            S.op("S", "activation", (negD_sb[0:16, o:o + HT], wka_sb[0:16, o:o + HT],
                                     AF.Identity), dict(scale=-1.0),
                 reads=("wka",), writes=("negD",))

        if debug and stage >= 1:
            S.op("Y", "dma_start", kw=dict(out=dbg["q"].ap(), in_=q_sb[:, :]), reads=("q",),
                 grp="dbg")
            S.op("Y", "dma_start", kw=dict(out=dbg["klat"].ap(), in_=klat_sb[:, :]),
                 reads=("klat",), grp="dbg")
            S.op("Y", "dma_start", kw=dict(out=dbg["vlat"].ap(), in_=vlat_sb[:, :]),
                 reads=("vlat",), grp="dbg")
            S.op("Y", "dma_start", kw=dict(out=dbg["negD"].ap()[0:16, :], in_=negD_sb[:, :]),
                 reads=("negD",), grp="dbg")

        s1.close()
        f1 = S.fence()

        if stage < 2:
            s12.close()
            finish()
            return nc

        attn_stk = ExitStack()
        attn_sb = attn_stk.enter_context(
            nc.sbuf_tensor("attn_sb", [128, 8 * TOK], BF16, side="right"))
        S.new_res("attn", fence=f1)
        s2 = ExitStack()
        P_sb = [s2.enter_context(nc.sbuf_tensor(f"P{h}", [128, 488], BF16)) for h in range(16)]
        gbh_sb = [s2.enter_context(nc.sbuf_tensor(f"gbh{i}", [128, 488], BF16)) for i in range(2)]
        lnS_sb = [s2.enter_context(nc.sbuf_tensor(f"lnS{i}", [128, 488], F32)) for i in range(2)]
        for h in range(16):
            S.new_res(f"P{h}", fence=f1)
        for i in range(2):
            S.new_res(f"gbh{i}", fence=f1)
            S.new_res(f"lnS{i}", fence=f1)

        SC_B = (0, 1)
        SUM_B = 2
        AT_B = (3, 4)
        G_B = (5, 6)

        units = []
        for qi, (q0, q1) in enumerate(QUADS):
            subs = list(range(q0, q1))
            w = sum(SUBQ[s] for s in subs)
            mask = c_mq0 if qi == 0 else (c_mqs if qi < 4 else c_mlast)
            for h in range(16):
                units.append((qi, h, subs, w, mask))
        nU = len(units)

        def p_scores(u):
            qi, h, subs, w, mask = units[u]
            c, po = h // 2, 64 * (h % 2)
            sb = SC_B[u % 2]
            col = 0
            for s in subs:
                keys, qn = SUBK[s], SUBQ[s]
                S.op("T", "matmul",
                     (pb[sb][0:keys, col:col + qn],
                      klat_sb[po:po + 64, A * c + QS * s:A * c + QS * s + keys],
                      q_sb[po:po + 64, A * c + 3 + QS * s:A * c + 3 + QS * s + qn]),
                     dict(start=(s == subs[0]), stop=False, skip_group_check=True),
                     reads=("klat", "q"), writes=(PB[sb],))
                col += qn
            S.op("T", "matmul",
                 (pb[sb][0:128, 0:w], c_selbc[0:16, 128 * h:128 * (h + 1)],
                  negD_sb[0:16, QS * subs[0]:QS * subs[0] + w]),
                 dict(start=False, stop=True, skip_group_check=True),
                 reads=("consts", "negD"), writes=(PB[sb],))
            S.op("S", "activation",
                 (P_sb[h][:, 0:w], pb[sb][:, 0:w], AF.Exp), dict(scale=0.125),
                 reads=(PB[sb],), writes=(f"P{h}",))

        def p_mid(u):
            qi, h, subs, w, mask = units[u]
            mb = (2, 7)[u % 2]
            keys0 = 128 if subs[0] < 16 else 102
            S.op("V", "tensor_tensor",
                 (P_sb[h][:, 0:w], P_sb[h][:, 0:w], mask[:, 0:w], ALU.mult),
                 reads=(f"P{h}", "consts"), writes=(f"P{h}",))
            S.op("T", "matmul",
                 (pb[mb][:, 0:w], c_ones128[0:keys0, :], P_sb[h][0:keys0, 0:w]),
                 dict(start=True, stop=True, skip_group_check=True),
                 reads=("consts", f"P{h}"), writes=(PB[mb],))
            S.op("S", "activation", (lnS_sb[u % 2][:, 0:w], pb[mb][:, 0:w], AF.Ln),
                 reads=(PB[mb],), writes=(f"lnS{u % 2}",))
            S.op("S", "activation", (gbh_sb[u % 2][:, 0:w], lnS_sb[u % 2][:, 0:w], AF.Exp),
                 dict(scale=-1.0, bias=c_b29[:, 0:1]),
                 reads=(f"lnS{u % 2}", "consts"), writes=(f"gbh{u % 2}",))

        def p_out(u):
            qi, h, subs, w, mask = units[u]
            c, po = h // 2, 64 * (h % 2)
            ab = AT_B[u % 2]
            S.op("V", "tensor_tensor",
                 (P_sb[h][:, 0:w], P_sb[h][:, 0:w], gbh_sb[u % 2][:, 0:w], ALU.mult),
                 reads=(f"P{h}", f"gbh{u % 2}"), writes=(f"P{h}",), latch=True)
            col = 0
            for s in subs:
                keys, qn = SUBK[s], SUBQ[s]
                S.op("T", "matmul",
                     (pb[ab][po:po + 64, col:col + qn],
                      vlat_sb[0:keys, E * s + 64 * h:E * s + 64 * (h + 1)],
                      P_sb[h][0:keys, col:col + qn]),
                     dict(start=(s == subs[0]), stop=(s == subs[-1]), skip_group_check=True),
                     reads=("vlat", f"P{h}"), writes=(PB[ab],))
                col += qn
            S.op("V", "tensor_copy",
                 (attn_sb[po:po + 64, TOK * c + QS * subs[0]:TOK * c + QS * subs[0] + w],
                  pb[ab][po:po + 64, 0:w]),
                 reads=(PB[ab],), writes=("attn",))

        for t in range(nU + 2):
            if t < nU:
                p_scores(t)
            if 0 <= t - 1 < nU:
                p_mid(t - 1)
            if 0 <= t - 2 < nU:
                p_out(t - 2)
                if debug and t - 2 <= 1:
                    qi, h, subs, w, mask = units[t - 2]
                    S.op("Y", "dma_start", kw=dict(out=dbg[f"P{h}"].ap(), in_=P_sb[h][:, :]),
                         reads=(f"P{h}",), grp="dbg")

        if debug and stage >= 2:
            S.op("Y", "dma_start", kw=dict(out=dbg["attn"].ap(), in_=attn_sb[:, :]),
                 reads=("attn",), grp="dbg")

        if stage < 3:
            s2.close()
            s12.close()
            attn_stk.close()
            finish()
            return nc

        s2.close()
        s12.close()
        f2 = S.fence()

        x1full_sb = EC(nc.sbuf_tensor("x1full_sb", [128, 16384], BF16))
        s3 = ExitStack()
        wproj_sb = s3.enter_context(nc.sbuf_tensor("wproj_sb", [128, 8 * 1024], BF16))
        xres_sb = [s3.enter_context(nc.sbuf_tensor(f"xres{i}", [128, 4096], F32)) for i in range(2)]
        z_sb = [s3.enter_context(nc.sbuf_tensor(f"z{i}", [128, 4096], F32)) for i in range(2)]
        zb_sb = [s3.enter_context(nc.sbuf_tensor(f"zb{i}", [128, 4096], BF16)) for i in range(2)]
        zsq_sb = [s3.enter_context(nc.sbuf_tensor(f"zsq{i}", [128, 512], BF16)) for i in range(2)]

        S.new_res("wproj", fence=f2)
        for i in range(2):
            for k in ("xres", "z", "zb", "zsq"):
                S.new_res(f"{k}{i}", fence=f2)
        S.new_res("x1full", fence=f2)
        S.new_res("stt3")

        for kc in range(8):
            S.op("G", "dma_start",
                 kw=dict(out=wproj_sb[:, 1024 * kc:1024 * (kc + 1)],
                         in_=d["wprojT"].ap()[128 * kc:128 * (kc + 1), :]),
                 writes=("wproj",), grp="wproj")

        def s3_proj(r):
            sl = r % 2
            S.op("Y", "dma_start",
                 kw=dict(out=xres_sb[sl][:, :].rearrange("p (c n) -> p c n", n=512),
                         in_=d["xT"].ap()[:, 3 + 512 * r:3 + 512 * (r + 1)].rearrange(
                             "(c p) n -> p c n", p=128)),
                 writes=(f"xres{sl}",), grp=f"xres{sl}")
            for fc in range(8):
                bank = (r * 8 + fc) % 2
                for kc in range(8):
                    S.op("T", "matmul",
                         (pb[bank][:, 0:512],
                          wproj_sb[:, 1024 * kc + 128 * fc:1024 * kc + 128 * (fc + 1)],
                          attn_sb[:, TOK * kc + 512 * r:TOK * kc + 512 * (r + 1)]),
                         dict(start=(kc == 0), stop=(kc == 7), skip_group_check=True),
                         reads=("wproj", "attn"), writes=(PB[bank],))
                S.op("V", "scalar_tensor_tensor",
                     (z_sb[sl][:, 512 * fc:512 * (fc + 1)], pb[bank][:, 0:512],
                      c_bproj[:, fc:fc + 1], xres_sb[sl][:, 512 * fc:512 * (fc + 1)],
                      ALU.add, ALU.add),
                     reads=(PB[bank], "consts", f"xres{sl}"), writes=(f"z{sl}",))
                S.op("S", "activation",
                     (zb_sb[sl][:, 512 * fc:512 * (fc + 1)],
                      z_sb[sl][:, 512 * fc:512 * (fc + 1)], AF.Copy),
                     reads=(f"z{sl}",), writes=(f"zb{sl}",))

        def s3_ln(r):
            sl = r % 2
            _ln(S, pb, PB, c_ones, c_onesrow, stt_sb, "stt3",
                z_sb[sl], f"z{sl}", (zb_sb[sl], [512 * c for c in range(8)]), f"zb{sl}",
                zsq_sb, ("zsq0", "zsq1"),
                x1full_sb, "x1full", 512,
                [512 * c for c in range(8)], (2, 3, 4, 5),
                dst_cols=[2048 * c + 512 * r for c in range(8)])

        for r in range(5):
            if r < 4:
                s3_proj(r)
            if r - 1 >= 0:
                s3_ln(r - 1)

        if debug and stage >= 3 and x1_hbm is not None:
            for r in range(4):
                S.op("G", "dma_start",
                     kw=dict(out=x1_hbm.ap()[:, 512 * r:512 * (r + 1)].rearrange(
                                 "(c p) n -> p c n", p=128),
                             in_=x1full_sb[:, :].rearrange("p (c n) -> p c n", n=2048)[
                                 :, :, 512 * r:512 * (r + 1)]),
                     reads=("x1full",))
        if stage < 4:
            s3.close()
            attn_stk.close()
            finish()
            return nc

        s3.close()
        attn_stk.close()
        f3 = S.fence()

        s4 = ExitStack()
        z2_sb = s4.enter_context(nc.sbuf_tensor("z2_sb", [128, 8192], F32))
        wg_sb = [s4.enter_context(nc.sbuf_tensor(f"wg{i}", [128, 4096], BF16)) for i in range(2)]
        wu_sb = [s4.enter_context(nc.sbuf_tensor(f"wu{i}", [128, 4096], BF16)) for i in range(2)]
        wd_sb = [s4.enter_context(nc.sbuf_tensor(f"wd{i}", [128, 2048], BF16)) for i in range(3)]
        gsil_sb = [s4.enter_context(nc.sbuf_tensor(f"gsil{i}", [128, 512], BF16)) for i in range(2)]
        ub_sb = [s4.enter_context(nc.sbuf_tensor(f"ub{i}", [128, 512], BF16)) for i in range(2)]
        a_sb = s4.enter_context(nc.sbuf_tensor("a_sb", [128, 24 * 1024], BF16))
        outh_sb = s4.enter_context(nc.sbuf_tensor("outh_sb", [128, 4096], F32))
        zsqb_sb = [s4.enter_context(nc.sbuf_tensor(f"zsqb{i}", [128, 512], BF16)) for i in range(2)]
        for i in range(2):
            for k in ("wg", "wu", "gsil", "ub", "zsqb"):
                S.new_res(f"{k}{i}", fence=f3)
        for i in range(3):
            S.new_res(f"wd{i}", fence=f3)
        for k in ("z2", "a", "outh"):
            S.new_res(k, fence=f3)

        def s4_gateup(r):
            for h in range(6):
                ws = h % 2
                S.op("G", "dma_start",
                     kw=dict(out=wg_sb[ws][:, :].rearrange("p (c f) -> p c f", f=512),
                             in_=d["wgT"].ap()[:, 512 * h:512 * (h + 1)].rearrange(
                                 "(c p) f -> p c f", p=128)),
                     writes=(f"wg{ws}",), grp=f"wg{ws}")
                S.op("G", "dma_start",
                     kw=dict(out=wu_sb[ws][:, :].rearrange("p (c f) -> p c f", f=512),
                             in_=d["wuT"].ap()[:, 512 * h:512 * (h + 1)].rearrange(
                                 "(c p) f -> p c f", p=128)),
                     writes=(f"wu{ws}",), grp=f"wu{ws}")
                for hf in range(4):
                    for half in range(2):
                        gbank = (hf * 2 + half) % 2
                        ubank = 2 + gbank
                        for kc in range(8):
                            rhs = x1full_sb[:, 2048 * kc + 1024 * r + 512 * half:
                                            2048 * kc + 1024 * r + 512 * (half + 1)]
                            S.op("T", "matmul",
                                 (pb[gbank][:, 0:512],
                                  wg_sb[ws][:, 512 * kc + 128 * hf:512 * kc + 128 * (hf + 1)],
                                  rhs),
                                 dict(start=(kc == 0), stop=(kc == 7), skip_group_check=True),
                                 reads=(f"wg{ws}", "x1full"), writes=(PB[gbank],))
                            S.op("T", "matmul",
                                 (pb[ubank][:, 0:512],
                                  wu_sb[ws][:, 512 * kc + 128 * hf:512 * kc + 128 * (hf + 1)],
                                  rhs),
                                 dict(start=(kc == 0), stop=(kc == 7), skip_group_check=True),
                                 reads=(f"wu{ws}", "x1full"), writes=(PB[ubank],))
                        gs = (hf * 2 + half) % 2
                        S.op("S", "activation",
                             (gsil_sb[gs][:, 0:512], pb[gbank][:, 0:512], AF.Silu),
                             dict(scale=INV127),
                             reads=(PB[gbank],), writes=(f"gsil{gs}",))
                        S.op("V", "tensor_scalar",
                             (ub_sb[gs][:, 0:512], pb[ubank][:, 0:512], INV127SQ, None, ALU.mult),
                             reads=(PB[ubank],), writes=(f"ub{gs}",))
                        ahc = 4 * h + hf
                        S.op("V", "tensor_tensor",
                             (a_sb[:, 1024 * ahc + 512 * half:1024 * ahc + 512 * (half + 1)],
                              gsil_sb[gs][:, 0:512], ub_sb[gs][:, 0:512], ALU.mult),
                             reads=(f"gsil{gs}", f"ub{gs}"), writes=("a",))

        def s4_down(r):
            if debug:
                S.op("Y", "dma_start", kw=dict(out=dbg["a4"].ap()[r], in_=a_sb[:, :]),
                     reads=("a",), grp="dbg")
                if r == 0:
                    S.op("Y", "dma_start", kw=dict(out=dbg["x1full"].ap(), in_=x1full_sb[:, :]),
                         reads=("x1full",), grp="dbg")
            for fcg in range(2):
                for hcp in range(12):
                    ws = hcp % 3
                    S.op("G", "dma_start",
                         kw=dict(out=wd_sb[ws][:, :].rearrange("p (c e) -> p c e", e=1024),
                                 in_=d["wdT"].ap()[256 * hcp:256 * (hcp + 1), :].rearrange(
                                     "(c p) e -> p c e", p=128)),
                         writes=(f"wd{ws}",), grp=f"wd{ws}")
                    for hi in range(2):
                        hc = 2 * hcp + hi
                        for fi in range(4):
                            fc = 4 * fcg + fi
                            for half in range(2):
                                bank = fi * 2 + half
                                S.op("T", "matmul",
                                     (pb[bank][:, 0:512],
                                      wd_sb[ws][:, 1024 * hi + 128 * fc:1024 * hi + 128 * (fc + 1)],
                                      a_sb[:, 1024 * hc + 512 * half:1024 * hc + 512 * (half + 1)]),
                                     dict(start=(hc == 0), stop=(hc == 23), skip_group_check=True),
                                     reads=(f"wd{ws}", "a"), writes=(PB[bank],))
                for fi in range(4):
                    fc = 4 * fcg + fi
                    for half in range(2):
                        bank = fi * 2 + half
                        cc = 1024 * fc + 512 * half
                        x1c = 2048 * fc + 1024 * r + 512 * half
                        S.op("V", "tensor_tensor",
                             (z2_sb[:, cc:cc + 512], pb[bank][:, 0:512],
                              x1full_sb[:, x1c:x1c + 512], ALU.add),
                             reads=(PB[bank], "x1full"), writes=("z2",))

        def s4_ln2(r):
            if debug:
                S.op("Y", "dma_start", kw=dict(out=dbg["z2"].ap()[r], in_=z2_sb[:, :]),
                     reads=("z2",), grp="dbg")
            for half in range(2):
                cols = [1024 * c + 512 * half for c in range(8)]
                for c in range(8):
                    S.op("S", "activation",
                         (a_sb[:, cols[c]:cols[c] + 512],
                          z2_sb[:, cols[c]:cols[c] + 512], AF.Copy),
                         reads=("z2",), writes=("a",))
                _ln(S, pb, PB, c_ones, c_onesrow, stt_sb, "stt3",
                    z2_sb, "z2", (a_sb, cols), "a",
                    zsqb_sb, ("zsqb0", "zsqb1"),
                    outh_sb, "outh", 512, cols, (4, 5, 6, 7),
                    dst_cols=[512 * c for c in range(8)])
                S.op("Y", "dma_start",
                     kw=dict(out=y.ap()[:, 1024 * r + 512 * half:1024 * r + 512 * (half + 1)]
                             .rearrange("(c p) n -> p c n", p=128),
                             in_=outh_sb[:, :].rearrange("p (c n) -> p c n", n=512)),
                     reads=("outh",), grp="yst")

        s4_gateup(0)
        s4_down(0)
        s4_ln2(0)
        s4_gateup(1)
        s4_down(1)
        s4_ln2(1)

        s4.close()
        finish()
    return nc



def _window_mask_blocks():
    std = np.zeros((128, 122), dtype=np.float32)
    for t in range(122):
        std[t:t + 7, t] = 1.0
    w0 = std.copy()
    w0[0:3, :] = 0.0
    w0[0, 0], w0[0, 1], w0[0, 2] = 3.0, 2.0, 1.0
    wl = np.zeros((128, 96), dtype=np.float32)
    for t in range(96):
        wl[t:t + 7, t] = 1.0
    wl[102:, :] = 0.0
    wl7 = wl.copy()
    wl7[99:102, :] = 0.0
    wl7[99, 93], wl7[99, 94], wl7[99, 95] = 1.0, 2.0, 3.0
    return std, w0, wl, wl7


_CACHE = {}


def kernel(L0_attn_x, L0_attn_W_qkv, L0_attn_b_qkv, L0_attn_W_lat, L0_attn_b_lat,
           L0_attn_W_proj, L0_attn_b_proj, L0_ffn_w_gate, L0_ffn_w_up, L0_ffn_w_down):
    bf = ml_dtypes.bfloat16
    x = np.ascontiguousarray(np.asarray(L0_attn_x, dtype=np.float32)[0])
    Wqkv = np.asarray(L0_attn_W_qkv, dtype=np.float32)
    bqkv = np.asarray(L0_attn_b_qkv, dtype=np.float32)
    Wlat = np.asarray(L0_attn_W_lat, dtype=np.float32)
    blat = np.asarray(L0_attn_b_lat, dtype=np.float32)
    Wproj = np.asarray(L0_attn_W_proj, dtype=np.float32)
    bproj = np.asarray(L0_attn_b_proj, dtype=np.float32)

    wq_lay = np.ascontiguousarray(Wqkv.T)
    wqkv_r = np.ascontiguousarray(wq_lay.reshape(8, 128, 24, 128).transpose(2, 0, 1, 3))
    wprojT = np.ascontiguousarray(Wproj.T)
    wlat2 = np.ascontiguousarray(np.vstack([Wlat.T, Wlat.T]))
    wgT = np.ascontiguousarray(np.asarray(L0_ffn_w_gate, dtype=np.int32).T)
    wuT = np.ascontiguousarray(np.asarray(L0_ffn_w_up, dtype=np.int32).T)
    wdT = np.ascontiguousarray(np.asarray(L0_ffn_w_down, dtype=np.int32).T)
    bqkv_col = np.ascontiguousarray(bqkv.reshape(24, 128).T)
    blat_col = np.ascontiguousarray(np.concatenate([blat, blat])[:, None])
    blat_b512 = np.ascontiguousarray(np.tile(blat, (128, 8)))
    bproj_col = np.ascontiguousarray(bproj.reshape(8, 128).T)
    ones_col = np.ones((128, 1), dtype=bf)
    ones_row_f32 = np.ones((128, 128), dtype=np.float32)
    ones_sel = np.zeros((128, 256), dtype=np.float32)
    for h in range(16):
        ones_sel[:, 16 * h + h] = 1.0
    ind = np.zeros((128, 128), dtype=np.float32)
    for c in range(8):
        ind[0:64, 16 * c + 2 * c] = 1.0
        ind[64:128, 16 * c + 2 * c + 1] = 1.0
    selbc = np.zeros((128, 16 * 128), dtype=np.float32)
    for h in range(16):
        selbc[h, 128 * h:128 * (h + 1)] = 1.0

    std, w0blk, wl_std, wl7 = _window_mask_blocks()
    mqs = np.zeros((128, 488), dtype=np.float32)
    for j in range(4):
        mqs[:, 122 * j:122 * (j + 1)] = std

    key = ("prog", STAGE, DEBUG)
    if key not in _CACHE:
        _CACHE[key] = build_program(STAGE, DEBUG)
    nc = _CACHE[key]

    in_maps = []
    for c in range(NCORE):
        base = 2048 * c
        rows = np.arange(base - 3, base + 2051)
        if c == 0:
            rows = rows.copy()
            rows[0] = SEQ - 1
            rows[1] = -1
            rows[2] = -1
        if c == NCORE - 1:
            rows = rows.copy()
            rows[2051] = 0
            rows[2052] = -1
            rows[2053] = -1
        xc = np.zeros((A, E), dtype=np.float32)
        valid = rows >= 0
        xc[valid] = x[rows[valid]]
        xT = np.ascontiguousarray(xc.T)

        mq0 = mqs.copy()
        mq0[:, 0:122] = w0blk if c == 0 else std
        mlast = wl7 if c == NCORE - 1 else wl_std

        in_maps.append(dict(
            xT=xT, wqkv_r=wqkv_r, wprojT=wprojT, wlat2=wlat2,
            wgT=wgT, wuT=wuT, wdT=wdT,
            bqkv_col=bqkv_col, blat_col=blat_col, blat_b512=blat_b512,
            bproj_col=bproj_col,
            mq0=mq0.astype(bf), mqs=mqs.astype(bf), mlast=mlast.astype(bf),
            ones_col=ones_col, ones_row_f32=ones_row_f32,
            ones_sel=ones_sel.astype(bf), ones128=np.full((128, 128), 2.0**29, dtype=bf),
            b29=np.full((128, 1), 29.0 * np.log(2.0), dtype=np.float32),
            ones_col_f32=np.ones((128, 1), dtype=np.float32),
            selbc=selbc.astype(bf), ind=ind.astype(bf),
        ))

    res = run_bass_kernel_spmd(nc, in_maps, core_ids=list(range(NCORE)))
    out = np.empty((1, SEQ, E), dtype=np.float32)
    for c in range(NCORE):
        out[0, 2048 * c:2048 * (c + 1), :] = np.asarray(res.results[c]["y"]).T
    if DEBUG:
        kernel._last = res
    return out


# version 15 (final)
# speedup vs baseline: 1.0814x; 1.0350x over previous
import numpy as np
import ml_dtypes
from contextlib import ExitStack

import concourse.bass as bass
import concourse.mybir as mybir
from concourse.bass_utils import run_bass_kernel_spmd

F32 = mybir.dt.float32
BF16 = mybir.dt.bfloat16
I32 = mybir.dt.int32
AF = mybir.ActivationFunctionType
ALU = mybir.AluOpType
AX = mybir.AxisListType

NCORE = 8
SEQ = 16384
E = 1024
NH = 16
DH = 64
HID = 3072
TOK = 2048
A = TOK + 6
QS = 122
NSUB = 17
SUBQ = [122] * 16 + [96]
SUBK = [128] * 16 + [102]
QUADS = [(0, 4), (4, 8), (8, 12), (12, 16), (16, 17)]
R5 = [(0, 512), (512, 1024), (1024, 1536), (1536, 2048), (2048, 2054)]
INV127 = 1.0 / 127.0
INV127SQ = INV127 * INV127

STAGE = 4
DEBUG = False
DBGQ = 0

ENGS = ("T", "V", "S", "Y", "G")
BLOCK_ATTR = {"T": "tensor", "V": "vector", "S": "scalar", "Y": "sync", "G": "gpsimd"}
DMA_ENGS = ("Y", "G")


class _Op:
    __slots__ = ("eng", "name", "args", "kw", "deps", "gdeps", "inc", "cum", "idx", "fragile",
                 "grp")


class Sched:
    def __init__(self):
        self.ops = {e: [] for e in ENGS}
        self.res = {}
        self.gcnt = {}
        self.glast = {}

    def new_res(self, key, fence=None):
        self.res[key] = {"w": None, "r": [], "f": list(fence or [])}

    def fence(self):
        return [(e, len(self.ops[e]) - 1) for e in ENGS if self.ops[e]]

    def op(self, eng, name, args=(), kw=None, reads=(), writes=(), extra_deps=(), latch=False,
           fragile=False, grp=None):
        idx = len(self.ops[eng])
        deps = {}
        gdeps = {}

        def add(d):
            de, di = d
            if di < 0 or (de == eng and di >= idx):
                return
            t = self.ops[de][di]
            if de in DMA_ENGS and t.name == "dma_start":
                g = t.grp
                c = self.gcnt[g]
                if gdeps.get(g, 0) < c:
                    gdeps[g] = c
                return
            if de in DMA_ENGS:
                return
            if de == eng and not latch and not (idx - di <= 2):
                return
            if deps.get(de, -1) < di:
                deps[de] = di

        for k in reads:
            r = self.res[k]
            for dd in r["f"]:
                add(dd)
            r["f"] = []
            if r["w"]:
                add(r["w"])
            r["r"].append((eng, idx))
        for k in writes:
            r = self.res[k]
            for dd in r["f"]:
                add(dd)
            r["f"] = []
            if r["w"]:
                add(r["w"])
            for dd in r["r"]:
                add(dd)
            r["r"] = []
            r["w"] = (eng, idx)
        for dd in extra_deps:
            add(dd)
        o = _Op()
        o.eng, o.name, o.args, o.kw = eng, name, args, (kw or {})
        o.deps, o.gdeps, o.inc, o.idx = deps, gdeps, False, idx
        o.fragile = fragile
        o.grp = None
        if eng in DMA_ENGS and name == "dma_start":
            assert grp is not None, "dma_start needs a semaphore group"
            o.grp = grp
            self.gcnt[grp] = self.gcnt.get(grp, 0) + 1
            self.glast[grp] = (eng, idx)
        for de, di in deps.items():
            self.ops[de][di].inc = True
        self.ops[eng].append(o)
        return (eng, idx)

    def finalize(self):
        for e in ENGS:
            c = 0
            for o in self.ops[e]:
                if o.grp is None and o.inc:
                    c += 1
                o.cum = c

    def emit(self, block, sems):
        self.finalize()

        def runner(e):
            def run(engobj):
                mw = {}
                for o in self.ops[e]:
                    for de in sorted(o.deps):
                        di = o.deps[de]
                        val = self.ops[de][di].cum
                        assert val > 0, f"{e} op {o.name} depends on un-inc'd {de}[{di}]"
                        if mw.get(de, -1) >= val:
                            continue
                        mw[de] = val
                        engobj.wait_ge(sems[de], val)
                    for g in sorted(o.gdeps):
                        val = 16 * o.gdeps[g]
                        if mw.get(g, -1) >= val:
                            continue
                        mw[g] = val
                        engobj.wait_ge(sems[g], val)
                    if o.name == "wait_only":
                        continue
                    inst = getattr(engobj, o.name)(*o.args, **o.kw)
                    if o.grp is not None:
                        inst.then_inc(sems[o.grp], 16)
                    elif o.inc:
                        inst.then_inc(sems[e], 1)
            return run

        for e in ENGS:
            getattr(block, BLOCK_ATTR[e])(runner(e))


def _ln(S, pb, PB, c_ones, c_onesrow, stt, stt_key, src_sb, src_key, zb, zb_key,
        zsq, zsq_keys, dst_sb, dst_key, n, cols, banks, dst_cols=None):
    if dst_cols is None:
        dst_cols = cols
    """LayerNorm over the feature dim (8 chunks x 128 partitions) for n tokens.

    zb must hold a bf16 copy of src (stats operand). Overwrites src in place
    (src - mean); writes the normalized fp32 result into dst.
    """
    sb1, sb2, bb1, bb2 = banks
    m_sb = stt[0:1, 0:n]
    e2_sb = stt[0:1, 512:512 + n]
    m2_sb = stt[0:1, 1024:1024 + n]
    var_sb = stt[0:1, 1536:1536 + n]
    lnv_sb = stt[0:1, 1024:1024 + n]
    rstd_sb = stt[0:1, 2048:2048 + n]
    for c in range(8):
        S.op("T", "matmul", (pb[sb1][0:1, 0:n], c_ones[:, 0:1], zb[0][:, zb[1][c]:zb[1][c] + n]),
             dict(start=(c == 0), stop=(c == 7), skip_group_check=True),
             reads=("consts", zb_key), writes=(PB[sb1],))
    for c in range(8):
        zs = zsq[c % 2]
        S.op("S", "activation", (zs[:, 0:n], zb[0][:, zb[1][c]:zb[1][c] + n], AF.Square),
             reads=(zb_key,), writes=(zsq_keys[c % 2],))
        S.op("T", "matmul", (pb[sb2][0:1, 0:n], c_ones[:, 0:1], zs[:, 0:n]),
             dict(start=(c == 0), stop=(c == 7), skip_group_check=True),
             reads=("consts", zsq_keys[c % 2]), writes=(PB[sb2],))
    S.op("V", "tensor_scalar", (m_sb, pb[sb1][0:1, 0:n], 1.0 / E, None, ALU.mult),
         reads=(PB[sb1],), writes=(stt_key,))
    S.op("V", "tensor_scalar", (e2_sb, pb[sb2][0:1, 0:n], 1.0 / E, None, ALU.mult),
         reads=(PB[sb2],), writes=(stt_key,))
    S.op("V", "tensor_tensor", (m2_sb, m_sb, m_sb, ALU.mult), reads=(stt_key,), writes=(stt_key,))
    S.op("V", "tensor_tensor", (var_sb, e2_sb, m2_sb, ALU.subtract), reads=(stt_key,), writes=(stt_key,))
    S.op("V", "tensor_scalar", (var_sb, var_sb, 1e-5, None, ALU.add),
         reads=(stt_key,), writes=(stt_key,))
    S.op("S", "activation", (lnv_sb, var_sb, AF.Ln),
         reads=(stt_key,), writes=(stt_key,))
    S.op("S", "activation", (rstd_sb, lnv_sb, AF.Exp), dict(scale=-0.5),
         reads=(stt_key,), writes=(stt_key,))
    S.op("T", "matmul", (pb[bb1][:, 0:n], c_onesrow[0:1, :], m_sb),
         dict(start=True, stop=True, skip_group_check=True),
         reads=("consts", stt_key), writes=(PB[bb1],))
    S.op("T", "matmul", (pb[bb2][:, 0:n], c_onesrow[0:1, :], rstd_sb),
         dict(start=True, stop=True, skip_group_check=True),
         reads=("consts", stt_key), writes=(PB[bb2],))
    S.op("T", "matmul", (pb[sb1][0:1, 0:1], c_ones[0:1, 0:1], c_ones[0:1, 0:1]),
         dict(start=True, stop=True, skip_group_check=True),
         reads=("consts",), writes=(PB[bb1], PB[bb2], PB[sb1]))
    prev = None
    for c in range(8):
        cc = cols[c]
        S.op("V", "tensor_tensor",
             (src_sb[:, cc:cc + n], src_sb[:, cc:cc + n], pb[bb1][:, 0:n], ALU.subtract),
             reads=(src_key, PB[bb1]), writes=(src_key,))
        if prev is not None:
            S.op("V", "tensor_tensor",
                 (dst_sb[:, prev[1]:prev[1] + n], src_sb[:, prev[0]:prev[0] + n],
                  pb[bb2][:, 0:n], ALU.mult),
                 reads=(src_key, PB[bb2]), writes=(dst_key,))
        prev = (cc, dst_cols[c])
    S.op("V", "tensor_tensor",
         (dst_sb[:, prev[1]:prev[1] + n], src_sb[:, prev[0]:prev[0] + n],
          pb[bb2][:, 0:n], ALU.mult),
         reads=(src_key, PB[bb2]), writes=(dst_key,))


def build_program(stage=None, debug=None):
    stage = STAGE if stage is None else stage
    debug = DEBUG if debug is None else debug
    nc = bass.Bass("TRN2", target_bir_lowering=False)

    d = {}
    def din(name, shape, dt):
        d[name] = nc.dram_tensor(name, list(shape), dt, kind="ExternalInput")

    din("xT", (E, A), F32)
    din("wqkv_r", (24, 8, 128, 128), F32)
    din("wprojT", (E, E), F32)
    din("wlat2", (128, DH), F32)
    din("wgT", (E, HID), I32)
    din("wuT", (E, HID), I32)
    din("wdT", (HID, E), I32)
    din("bqkv_col", (128, 24), F32)
    din("blat_col", (128, 1), F32)
    din("blat_b512", (128, 512), F32)
    din("bproj_col", (128, 8), F32)
    din("mq0", (128, 488), BF16)
    din("mqs", (128, 488), BF16)
    din("mlast", (128, 96), BF16)
    din("ones_col", (128, 1), BF16)
    din("ones_row_f32", (128, 128), F32)
    din("ones_sel", (128, 256), BF16)
    din("ones128", (128, 128), BF16)
    din("selbc", (128, 16 * 128), BF16)
    din("b29", (128, 1), F32)
    din("ones_col_f32", (128, 1), F32)
    din("ind", (128, 128), BF16)

    x1_hbm = nc.dram_tensor("x1_dbg", [E, TOK], F32, kind="ExternalOutput") \
        if (debug and stage >= 3) else None
    y = nc.dram_tensor("y", [E, TOK], F32, kind="ExternalOutput")
    dbg = {}
    if debug:
        if stage >= 1:
            dbg["q"] = nc.dram_tensor("dbg_q", [128, 8 * A], BF16, kind="ExternalOutput")
            dbg["klat"] = nc.dram_tensor("dbg_klat", [128, 8 * A], BF16, kind="ExternalOutput")
            dbg["vlat"] = nc.dram_tensor("dbg_vlat", [128, NSUB * E], BF16, kind="ExternalOutput")
            dbg["negD"] = nc.dram_tensor("dbg_negD", [128, TOK], BF16, kind="ExternalOutput")
        if stage >= 2:
            dbg["attn"] = nc.dram_tensor("dbg_attn", [128, 8 * TOK], BF16, kind="ExternalOutput")
        if stage >= 4:
            dbg["x1full"] = nc.dram_tensor("dbg_x1full", [128, 16384], BF16, kind="ExternalOutput")
            dbg["z2"] = nc.dram_tensor("dbg_z2", [2, 128, 8192], F32, kind="ExternalOutput")
            dbg["a4"] = nc.dram_tensor("dbg_a4", [2, 128, 24576], BF16, kind="ExternalOutput")
            dbg["P0"] = nc.dram_tensor("dbg_P0", [128, 488], BF16, kind="ExternalOutput")
            dbg["P1"] = nc.dram_tensor("dbg_P1", [128, 488], BF16, kind="ExternalOutput")


    S = Sched()
    ctx = ExitStack()
    EC = ctx.enter_context

    with ctx:
        pb = [EC(nc.psum_tensor(f"pb{i}", [128, 512], F32)) for i in range(8)]
        PB = [f"pb{i}" for i in range(8)]
        for i in range(8):
            S.new_res(PB[i])

        sems = {e: EC(nc.semaphore(f"sem{e}")) for e in ENGS}
        block = EC(nc.Block())

        def finish():
            S.op("Y", "wait_only", extra_deps=list(S.glast.values()))
            for g in S.gcnt:
                sems[g] = EC(nc.semaphore(f"sg_{g}"))
            with nc.allow_low_precision("bf16 matmul pipeline with fp32 accumulation by design"):
                S.emit(block, sems)

        c_bqkv = EC(nc.sbuf_tensor("c_bqkv", [128, 24], F32))
        c_blat = EC(nc.sbuf_tensor("c_blat", [128, 1], F32))
        c_blat512 = EC(nc.sbuf_tensor("c_blat512", [128, 512], F32))
        c_bproj = EC(nc.sbuf_tensor("c_bproj", [128, 8], F32))
        c_mq0 = EC(nc.sbuf_tensor("c_mq0", [128, 488], BF16))
        c_mqs = EC(nc.sbuf_tensor("c_mqs", [128, 488], BF16))
        c_mlast = EC(nc.sbuf_tensor("c_mlast", [128, 96], BF16))
        c_ones = EC(nc.sbuf_tensor("c_ones", [128, 1], BF16))
        c_onesrow = EC(nc.sbuf_tensor("c_onesrow", [128, 128], F32))
        c_osel = EC(nc.sbuf_tensor("c_osel", [128, 256], BF16))
        c_ones128 = EC(nc.sbuf_tensor("c_ones128", [128, 128], BF16))
        c_selbc = EC(nc.sbuf_tensor("c_selbc", [128, 16 * 128], BF16))
        c_b29 = EC(nc.sbuf_tensor("c_b29", [128, 1], F32))
        c_onesf = EC(nc.sbuf_tensor("c_onesf", [128, 1], F32))
        stt_sb = EC(nc.sbuf_tensor("stt_sb", [1, 5 * 512], F32))
        c_ind = EC(nc.sbuf_tensor("c_ind", [128, 128], BF16))
        c_wlat = EC(nc.sbuf_tensor("c_wlat", [128, DH], BF16))
        S.new_res("consts")
        for name, t in [("bqkv_col", c_bqkv), ("blat_col", c_blat), ("blat_b512", c_blat512),
                        ("bproj_col", c_bproj), ("mq0", c_mq0), ("mqs", c_mqs),
                        ("mlast", c_mlast), ("ones_col", c_ones), ("ones_row_f32", c_onesrow),
                        ("ones_sel", c_osel), ("ones128", c_ones128), ("selbc", c_selbc),
                        ("b29", c_b29), ("ones_col_f32", c_onesf), ("ind", c_ind)]:
            S.op("Y", "dma_start", kw=dict(out=t[:, :], in_=d[name].ap()), writes=("consts",),
                 grp="cY")
        S.op("G", "dma_start", kw=dict(out=c_wlat[:, :], in_=d["wlat2"].ap()), writes=("consts",),
             grp="cG")

        def hot(k=1):
            for _ in range(k):
                S.op("T", "matmul", (pb[7][:, 0:512], c_selbc[:, 0:128], c_selbc[:, 0:512]),
                     dict(start=True, stop=True, skip_group_check=True), reads=("consts",))

        hot(192)

        s12 = ExitStack()
        q_sb = s12.enter_context(nc.sbuf_tensor("q_sb", [128, 8 * A], BF16))
        klat_sb = s12.enter_context(nc.sbuf_tensor("klat_sb", [128, 8 * A], BF16))
        vlat_sb = s12.enter_context(nc.sbuf_tensor("vlat_sb", [128, NSUB * E], BF16))
        qn2_sb = s12.enter_context(nc.sbuf_tensor("qn2_sb", [16, A], F32))
        kn2_sb = s12.enter_context(nc.sbuf_tensor("kn2_sb", [16, A], F32))
        wka_sb = s12.enter_context(nc.sbuf_tensor("wka_sb", [16, A], F32))
        negD_sb = s12.enter_context(nc.sbuf_tensor("negD_sb", [16, TOK], BF16))
        for k in ("q", "klat", "vlat", "qn2", "kn2", "wka", "negD"):
            S.new_res(k)

        s1 = ExitStack()
        xb_sb = [s1.enter_context(nc.sbuf_tensor(f"xb{c}", [128, A], BF16)) for c in range(8)]
        wq_sb = [s1.enter_context(nc.sbuf_tensor(f"wq{i}", [128, 1024], BF16)) for i in range(2)]
        kfm_sb = [s1.enter_context(nc.sbuf_tensor(f"kfm{i}", [128, A], BF16)) for i in range(2)]
        vfm_sb = [s1.enter_context(nc.sbuf_tensor(f"vfm{i}", [128, A], BF16)) for i in range(2)]
        sq_sb = [s1.enter_context(nc.sbuf_tensor(f"sq{i}", [128, 512], BF16)) for i in range(2)]
        for c in range(8):
            S.new_res(f"xb{c}")
        for i in range(2):
            for k in ("wq", "kfm", "vfm", "sq"):
                S.new_res(f"{k}{i}")


        def wq_load(fc):
            slot = fc % 2
            src = d["wqkv_r"].ap()[fc:fc + 1].rearrange("o c p f -> p (o c) f")
            S.op("G", "dma_start",
                 kw=dict(out=wq_sb[slot][:, :].rearrange("p (c f) -> p c f", f=128), in_=src),
                 writes=(f"wq{slot}",), grp=f"wq{slot}")

        wq_load(0)
        wq_load(1)
        for c in range(8):
            S.op("G", "dma_start",
                 kw=dict(out=xb_sb[c][:, :], in_=d["xT"].ap()[128 * c:128 * (c + 1), :]),
                 writes=(f"xb{c}",), grp="xb")

        KE_B, KO_B, VE_B, VO_B, NRM_B = 2, 3, 4, 5, 6

        def norm_pass(src_sb, src_key, dst_sb, dst_key):
            jobsn = [(ri, c) for ri in range(5) for c in range(8)]

            def n_sq(i):
                ri, c = jobsn[i]
                r0, r1 = R5[ri]
                S.op("S", "activation",
                     (sq_sb[i % 2][:, 0:r1 - r0], src_sb[:, A * c + r0:A * c + r1], AF.Square),
                     reads=(src_key,), writes=(f"sq{i % 2}",))

            def n_mm(i):
                ri, c = jobsn[i]
                r0, r1 = R5[ri]
                n = r1 - r0
                S.op("T", "matmul",
                     (pb[NRM_B][0:16, 0:n], c_ind[:, 16 * c:16 * (c + 1)], sq_sb[i % 2][:, 0:n]),
                     dict(start=(c == 0), stop=(c == 7), skip_group_check=True),
                     reads=("consts", f"sq{i % 2}"), writes=(PB[NRM_B],))
                if c == 7:
                    S.op("V", "tensor_copy", (dst_sb[0:16, r0:r1], pb[NRM_B][0:16, 0:n]),
                         reads=(PB[NRM_B],), writes=(dst_key,))

            for i in range(41):
                if i < 40:
                    n_sq(i)
                if i - 1 >= 0:
                    n_mm(i - 1)

        jobs1 = [(fc, ri) for fc in range(24) for ri in range(5)]

        def s1_group(j):
            fc, ri = jobs1[j]
            slot = fc % 2
            r0, r1 = R5[ri]
            n = r1 - r0
            bank = j % 2
            for kc in range(8):
                S.op("T", "matmul",
                     (pb[bank][:, 0:n], wq_sb[slot][:, 128 * kc:128 * (kc + 1)],
                      xb_sb[kc][:, r0:r1]),
                     dict(start=(kc == 0), stop=(kc == 7), skip_group_check=True),
                     reads=(f"wq{slot}", f"xb{kc}"), writes=(PB[bank],))
            if ri == 4 and fc + 2 < 24:
                wq_load(fc + 2)

        def s1_evac(j):
            fc, ri = jobs1[j]
            slot = fc % 2
            r0, r1 = R5[ri]
            n = r1 - r0
            bank = j % 2
            kind = 0 if fc < 8 else (1 if fc < 16 else 2)
            c = fc % 8
            if kind == 0:
                dst_sb, dst_key, dcol = q_sb, "q", A * c
            elif kind == 1:
                dst_sb, dst_key, dcol = kfm_sb[slot], f"kfm{slot}", 0
            else:
                dst_sb, dst_key, dcol = vfm_sb[slot], f"vfm{slot}", 0
            S.op("S", "activation",
                 (dst_sb[:, dcol + r0:dcol + r1], pb[bank][:, 0:n], AF.Identity),
                 dict(bias=c_bqkv[:, fc:fc + 1]),
                 reads=(PB[bank], "consts"), writes=(dst_key,))

        def s1_follow(j):
            fc, ri = jobs1[j]
            slot = fc % 2
            r0, r1 = R5[ri]
            n = r1 - r0
            kind = 0 if fc < 8 else (1 if fc < 16 else 2)
            c = fc % 8
            if kind == 1:
                S.op("T", "matmul",
                     (pb[KE_B][0:64, 0:n], c_wlat[0:64, :], kfm_sb[slot][0:64, r0:r1]),
                     dict(start=True, stop=True, skip_group_check=True),
                     reads=("consts", f"kfm{slot}"), writes=(PB[KE_B],))
                S.op("T", "matmul",
                     (pb[KO_B][64:128, 0:n], c_wlat[64:128, :], kfm_sb[slot][64:128, r0:r1]),
                     dict(start=True, stop=True, skip_group_check=True),
                     reads=("consts", f"kfm{slot}"), writes=(PB[KO_B],))
                S.op("S", "activation",
                     (klat_sb[0:64, A * c + r0:A * c + r1], pb[KE_B][0:64, 0:n], AF.Identity),
                     dict(bias=c_blat[0:64, 0:1]),
                     reads=(PB[KE_B], "consts"), writes=("klat",))
                S.op("S", "activation",
                     (klat_sb[64:128, A * c + r0:A * c + r1], pb[KO_B][64:128, 0:n], AF.Identity),
                     dict(bias=c_blat[64:128, 0:1]),
                     reads=(PB[KO_B], "consts"), writes=("klat",))
            if kind == 2 and ri == 4:
                for sg0, sg1 in ((0, 8), (8, 16), (16, 17)):
                    glen = sg1 - sg0
                    for par, bank in ((0, VE_B), (1, VO_B)):
                        p0 = 64 * par
                        for jj, s in enumerate(range(sg0, sg1)):
                            wlen = 128 if s < 16 else 102
                            S.op("T", "matmul",
                                 (pb[bank][0:wlen, 64 * jj:64 * (jj + 1)],
                                  vfm_sb[slot][p0:p0 + 64, QS * s:QS * s + wlen],
                                  c_wlat[p0:p0 + 64, :]),
                                 dict(start=(jj == 0), stop=(jj == glen - 1), skip_group_check=True),
                                 reads=(f"vfm{slot}", "consts"), writes=(PB[bank],))
                        dst = vlat_sb[:, :].rearrange("p (s e) -> p s e", e=E)[
                            :, sg0:sg1, 128 * c + 64 * par:128 * c + 64 * par + 64]
                        S.op("V", "tensor_tensor",
                             (dst,
                              pb[bank][:, 0:64 * glen].rearrange("p (s e) -> p s e", e=64),
                              c_blat512[:, 0:64 * glen].rearrange("p (s e) -> p s e", e=64),
                              ALU.add),
                             reads=(PB[bank], "consts"), writes=("vlat",))
            if fc == 7 and ri == 4:
                norm_pass(q_sb, "q", qn2_sb, "qn2")
            if fc == 15 and ri == 4:
                norm_pass(klat_sb, "klat", kn2_sb, "kn2")

        for j in range(122):
            if j < 120:
                s1_group(j)
            if 0 <= j - 1 < 120:
                s1_evac(j - 1)
            if 0 <= j - 2 < 120:
                s1_follow(j - 2)

        S.op("V", "tensor_tensor", (wka_sb[0:16, 0:A - 1], kn2_sb[0:16, 0:A - 1],
                                    kn2_sb[0:16, 1:A], ALU.max),
             reads=("kn2",), writes=("wka",))
        S.op("V", "tensor_tensor", (kn2_sb[0:16, 0:A - 3], wka_sb[0:16, 0:A - 3],
                                    wka_sb[0:16, 2:A - 1], ALU.max),
             reads=("wka", "kn2"), writes=("kn2",))
        S.op("V", "tensor_tensor", (wka_sb[0:16, 0:TOK], kn2_sb[0:16, 0:TOK],
                                    kn2_sb[0:16, 3:TOK + 3], ALU.max),
             reads=("kn2",), writes=("wka",))
        HT = TOK // 2
        for hh in range(2):
            o = HT * hh
            S.op("V", "tensor_scalar", (wka_sb[0:16, o:o + HT], wka_sb[0:16, o:o + HT],
                                        1e-30, None, ALU.add),
                 reads=("wka",), writes=("wka",))
            S.op("V", "tensor_scalar", (qn2_sb[0:16, 3 + o:3 + o + HT],
                                        qn2_sb[0:16, 3 + o:3 + o + HT], 1e-30, None, ALU.add),
                 reads=("qn2",), writes=("qn2",))
        for hh in range(2):
            o = HT * hh
            S.op("S", "activation", (kn2_sb[0:16, o:o + HT], wka_sb[0:16, o:o + HT], AF.Ln),
                 reads=("wka",), writes=("kn2",))
            S.op("S", "activation", (wka_sb[0:16, o:o + HT], qn2_sb[0:16, 3 + o:3 + o + HT],
                                     AF.Ln),
                 reads=("qn2",), writes=("wka",))
        for hh in range(2):
            o = HT * hh
            S.op("V", "tensor_tensor", (qn2_sb[0:16, o:o + HT], wka_sb[0:16, o:o + HT],
                                        kn2_sb[0:16, o:o + HT], ALU.add),
                 reads=("wka", "kn2"), writes=("qn2",))
        for hh in range(2):
            o = HT * hh
            S.op("S", "activation", (wka_sb[0:16, o:o + HT], qn2_sb[0:16, o:o + HT], AF.Exp),
                 dict(scale=0.5), reads=("qn2",), writes=("wka",))
        for hh in range(2):
            o = HT * hh
            S.op("S", "activation", (negD_sb[0:16, o:o + HT], wka_sb[0:16, o:o + HT],
                                     AF.Identity), dict(scale=-1.0),
                 reads=("wka",), writes=("negD",))

        if debug and stage >= 1:
            S.op("Y", "dma_start", kw=dict(out=dbg["q"].ap(), in_=q_sb[:, :]), reads=("q",),
                 grp="dbg")
            S.op("Y", "dma_start", kw=dict(out=dbg["klat"].ap(), in_=klat_sb[:, :]),
                 reads=("klat",), grp="dbg")
            S.op("Y", "dma_start", kw=dict(out=dbg["vlat"].ap(), in_=vlat_sb[:, :]),
                 reads=("vlat",), grp="dbg")
            S.op("Y", "dma_start", kw=dict(out=dbg["negD"].ap()[0:16, :], in_=negD_sb[:, :]),
                 reads=("negD",), grp="dbg")

        s1.close()
        f1 = S.fence()

        if stage < 2:
            s12.close()
            finish()
            return nc

        attn_stk = ExitStack()
        attn_sb = attn_stk.enter_context(
            nc.sbuf_tensor("attn_sb", [128, 8 * TOK], BF16, side="right"))
        S.new_res("attn", fence=f1)
        s2 = ExitStack()
        P_sb = [s2.enter_context(nc.sbuf_tensor(f"P{h}", [128, 488], BF16)) for h in range(16)]
        gbh_sb = [s2.enter_context(nc.sbuf_tensor(f"gbh{i}", [128, 488], BF16)) for i in range(2)]
        lnS_sb = [s2.enter_context(nc.sbuf_tensor(f"lnS{i}", [128, 488], F32)) for i in range(2)]
        for h in range(16):
            S.new_res(f"P{h}", fence=f1)
        for i in range(2):
            S.new_res(f"gbh{i}", fence=f1)
            S.new_res(f"lnS{i}", fence=f1)

        SC_B = (0, 1)
        SUM_B = 2
        AT_B = (3, 4)
        G_B = (5, 6)

        units = []
        for qi, (q0, q1) in enumerate(QUADS):
            subs = list(range(q0, q1))
            w = sum(SUBQ[s] for s in subs)
            mask = c_mq0 if qi == 0 else (c_mqs if qi < 4 else c_mlast)
            for h in range(16):
                units.append((qi, h, subs, w, mask))
        nU = len(units)

        def p_scores(u):
            qi, h, subs, w, mask = units[u]
            c, po = h // 2, 64 * (h % 2)
            sb = SC_B[u % 2]
            col = 0
            for s in subs:
                keys, qn = SUBK[s], SUBQ[s]
                S.op("T", "matmul",
                     (pb[sb][0:keys, col:col + qn],
                      klat_sb[po:po + 64, A * c + QS * s:A * c + QS * s + keys],
                      q_sb[po:po + 64, A * c + 3 + QS * s:A * c + 3 + QS * s + qn]),
                     dict(start=(s == subs[0]), stop=False, skip_group_check=True),
                     reads=("klat", "q"), writes=(PB[sb],))
                col += qn
            S.op("T", "matmul",
                 (pb[sb][0:128, 0:w], c_selbc[0:16, 128 * h:128 * (h + 1)],
                  negD_sb[0:16, QS * subs[0]:QS * subs[0] + w]),
                 dict(start=False, stop=True, skip_group_check=True),
                 reads=("consts", "negD"), writes=(PB[sb],))
            S.op("S", "activation",
                 (P_sb[h][:, 0:w], pb[sb][:, 0:w], AF.Exp), dict(scale=0.125),
                 reads=(PB[sb],), writes=(f"P{h}",))

        def p_mid(u):
            qi, h, subs, w, mask = units[u]
            mb = (2, 7)[u % 2]
            keys0 = 128 if subs[0] < 16 else 102
            S.op("V", "tensor_tensor",
                 (P_sb[h][:, 0:w], P_sb[h][:, 0:w], mask[:, 0:w], ALU.mult),
                 reads=(f"P{h}", "consts"), writes=(f"P{h}",))
            S.op("T", "matmul",
                 (pb[mb][:, 0:w], c_ones128[0:keys0, :], P_sb[h][0:keys0, 0:w]),
                 dict(start=True, stop=True, skip_group_check=True),
                 reads=("consts", f"P{h}"), writes=(PB[mb],))
            S.op("S", "activation", (lnS_sb[u % 2][:, 0:w], pb[mb][:, 0:w], AF.Ln),
                 reads=(PB[mb],), writes=(f"lnS{u % 2}",))
            S.op("S", "activation", (gbh_sb[u % 2][:, 0:w], lnS_sb[u % 2][:, 0:w], AF.Exp),
                 dict(scale=-1.0, bias=c_b29[:, 0:1]),
                 reads=(f"lnS{u % 2}", "consts"), writes=(f"gbh{u % 2}",))

        def p_out(u):
            qi, h, subs, w, mask = units[u]
            c, po = h // 2, 64 * (h % 2)
            ab = AT_B[u % 2]
            S.op("V", "tensor_tensor",
                 (P_sb[h][:, 0:w], P_sb[h][:, 0:w], gbh_sb[u % 2][:, 0:w], ALU.mult),
                 reads=(f"P{h}", f"gbh{u % 2}"), writes=(f"P{h}",), latch=True)
            col = 0
            for s in subs:
                keys, qn = SUBK[s], SUBQ[s]
                S.op("T", "matmul",
                     (pb[ab][po:po + 64, col:col + qn],
                      vlat_sb[0:keys, E * s + 64 * h:E * s + 64 * (h + 1)],
                      P_sb[h][0:keys, col:col + qn]),
                     dict(start=(s == subs[0]), stop=(s == subs[-1]), skip_group_check=True),
                     reads=("vlat", f"P{h}"), writes=(PB[ab],))
                col += qn
            S.op("V", "tensor_copy",
                 (attn_sb[po:po + 64, TOK * c + QS * subs[0]:TOK * c + QS * subs[0] + w],
                  pb[ab][po:po + 64, 0:w]),
                 reads=(PB[ab],), writes=("attn",))

        for t in range(nU + 2):
            if t < nU:
                p_scores(t)
            if 0 <= t - 1 < nU:
                p_mid(t - 1)
            if 0 <= t - 2 < nU:
                p_out(t - 2)
                if debug and t - 2 <= 1:
                    qi, h, subs, w, mask = units[t - 2]
                    S.op("Y", "dma_start", kw=dict(out=dbg[f"P{h}"].ap(), in_=P_sb[h][:, :]),
                         reads=(f"P{h}",), grp="dbg")

        if debug and stage >= 2:
            S.op("Y", "dma_start", kw=dict(out=dbg["attn"].ap(), in_=attn_sb[:, :]),
                 reads=("attn",), grp="dbg")

        if stage < 3:
            s2.close()
            s12.close()
            attn_stk.close()
            finish()
            return nc

        s2.close()
        s12.close()
        f2 = S.fence()

        x1full_sb = EC(nc.sbuf_tensor("x1full_sb", [128, 16384], BF16))
        s3 = ExitStack()
        wproj_sb = s3.enter_context(nc.sbuf_tensor("wproj_sb", [128, 8 * 1024], BF16))
        xres_sb = [s3.enter_context(nc.sbuf_tensor(f"xres{i}", [128, 4096], F32)) for i in range(2)]
        z_sb = [s3.enter_context(nc.sbuf_tensor(f"z{i}", [128, 4096], F32)) for i in range(2)]
        zb_sb = [s3.enter_context(nc.sbuf_tensor(f"zb{i}", [128, 4096], BF16)) for i in range(2)]
        zsq_sb = [s3.enter_context(nc.sbuf_tensor(f"zsq{i}", [128, 512], BF16)) for i in range(2)]

        S.new_res("wproj", fence=f2)
        for i in range(2):
            for k in ("xres", "z", "zb", "zsq"):
                S.new_res(f"{k}{i}", fence=f2)
        S.new_res("x1full", fence=f2)
        S.new_res("stt3")

        for kc in range(8):
            S.op("G", "dma_start",
                 kw=dict(out=wproj_sb[:, 1024 * kc:1024 * (kc + 1)],
                         in_=d["wprojT"].ap()[128 * kc:128 * (kc + 1), :]),
                 writes=("wproj",), grp="wproj")

        def s3_proj(r):
            sl = r % 2
            S.op("Y", "dma_start",
                 kw=dict(out=xres_sb[sl][:, :].rearrange("p (c n) -> p c n", n=512),
                         in_=d["xT"].ap()[:, 3 + 512 * r:3 + 512 * (r + 1)].rearrange(
                             "(c p) n -> p c n", p=128)),
                 writes=(f"xres{sl}",), grp=f"xres{sl}")
            for fp in range(4):
                for kc in range(8):
                    for bank, fc in ((0, 2 * fp), (1, 2 * fp + 1)):
                        S.op("T", "matmul",
                             (pb[bank][:, 0:512],
                              wproj_sb[:, 1024 * kc + 128 * fc:1024 * kc + 128 * (fc + 1)],
                              attn_sb[:, TOK * kc + 512 * r:TOK * kc + 512 * (r + 1)]),
                             dict(start=(kc == 0), stop=(kc == 7), skip_group_check=True),
                             reads=("wproj", "attn"), writes=(PB[bank],))
                for bank, fc in ((0, 2 * fp), (1, 2 * fp + 1)):
                    S.op("V", "scalar_tensor_tensor",
                         (z_sb[sl][:, 512 * fc:512 * (fc + 1)], pb[bank][:, 0:512],
                          c_bproj[:, fc:fc + 1], xres_sb[sl][:, 512 * fc:512 * (fc + 1)],
                          ALU.add, ALU.add),
                         reads=(PB[bank], "consts", f"xres{sl}"), writes=(f"z{sl}",))
                    S.op("S", "activation",
                         (zb_sb[sl][:, 512 * fc:512 * (fc + 1)],
                          z_sb[sl][:, 512 * fc:512 * (fc + 1)], AF.Copy),
                         reads=(f"z{sl}",), writes=(f"zb{sl}",))

        def s3_ln(r):
            sl = r % 2
            _ln(S, pb, PB, c_ones, c_onesrow, stt_sb, "stt3",
                z_sb[sl], f"z{sl}", (zb_sb[sl], [512 * c for c in range(8)]), f"zb{sl}",
                zsq_sb, ("zsq0", "zsq1"),
                x1full_sb, "x1full", 512,
                [512 * c for c in range(8)], (2, 3, 4, 5),
                dst_cols=[2048 * c + 512 * r for c in range(8)])

        for r in range(5):
            if r < 4:
                s3_proj(r)
            if r - 1 >= 0:
                s3_ln(r - 1)

        if debug and stage >= 3 and x1_hbm is not None:
            for r in range(4):
                S.op("G", "dma_start",
                     kw=dict(out=x1_hbm.ap()[:, 512 * r:512 * (r + 1)].rearrange(
                                 "(c p) n -> p c n", p=128),
                             in_=x1full_sb[:, :].rearrange("p (c n) -> p c n", n=2048)[
                                 :, :, 512 * r:512 * (r + 1)]),
                     reads=("x1full",))
        if stage < 4:
            s3.close()
            attn_stk.close()
            finish()
            return nc

        s3.close()
        attn_stk.close()
        f3 = S.fence()

        s4 = ExitStack()
        z2_sb = s4.enter_context(nc.sbuf_tensor("z2_sb", [128, 8192], F32))
        wg_sb = [s4.enter_context(nc.sbuf_tensor(f"wg{i}", [128, 4096], BF16)) for i in range(2)]
        wu_sb = [s4.enter_context(nc.sbuf_tensor(f"wu{i}", [128, 4096], BF16)) for i in range(2)]
        wd_sb = [s4.enter_context(nc.sbuf_tensor(f"wd{i}", [128, 2048], BF16)) for i in range(3)]
        gsil_sb = [s4.enter_context(nc.sbuf_tensor(f"gsil{i}", [128, 512], BF16)) for i in range(2)]
        ub_sb = [s4.enter_context(nc.sbuf_tensor(f"ub{i}", [128, 512], BF16)) for i in range(2)]
        a_sb = s4.enter_context(nc.sbuf_tensor("a_sb", [128, 24 * 1024], BF16))
        outh_sb = s4.enter_context(nc.sbuf_tensor("outh_sb", [128, 4096], F32))
        zsqb_sb = [s4.enter_context(nc.sbuf_tensor(f"zsqb{i}", [128, 512], BF16)) for i in range(2)]
        for i in range(2):
            for k in ("wg", "wu", "gsil", "ub", "zsqb"):
                S.new_res(f"{k}{i}", fence=f3)
        for i in range(3):
            S.new_res(f"wd{i}", fence=f3)
        for k in ("z2", "a", "outh"):
            S.new_res(k, fence=f3)

        def s4_gateup(r):
            for h in range(6):
                ws = h % 2
                S.op("G", "dma_start",
                     kw=dict(out=wg_sb[ws][:, :].rearrange("p (c f) -> p c f", f=512),
                             in_=d["wgT"].ap()[:, 512 * h:512 * (h + 1)].rearrange(
                                 "(c p) f -> p c f", p=128)),
                     writes=(f"wg{ws}",), grp=f"wg{ws}")
                S.op("G", "dma_start",
                     kw=dict(out=wu_sb[ws][:, :].rearrange("p (c f) -> p c f", f=512),
                             in_=d["wuT"].ap()[:, 512 * h:512 * (h + 1)].rearrange(
                                 "(c p) f -> p c f", p=128)),
                     writes=(f"wu{ws}",), grp=f"wu{ws}")
                for hf in range(4):
                    for half in range(2):
                        gbank = (hf * 2 + half) % 2
                        ubank = 2 + gbank
                        for kc in range(8):
                            rhs = x1full_sb[:, 2048 * kc + 1024 * r + 512 * half:
                                            2048 * kc + 1024 * r + 512 * (half + 1)]
                            S.op("T", "matmul",
                                 (pb[gbank][:, 0:512],
                                  wg_sb[ws][:, 512 * kc + 128 * hf:512 * kc + 128 * (hf + 1)],
                                  rhs),
                                 dict(start=(kc == 0), stop=(kc == 7), skip_group_check=True),
                                 reads=(f"wg{ws}", "x1full"), writes=(PB[gbank],))
                            S.op("T", "matmul",
                                 (pb[ubank][:, 0:512],
                                  wu_sb[ws][:, 512 * kc + 128 * hf:512 * kc + 128 * (hf + 1)],
                                  rhs),
                                 dict(start=(kc == 0), stop=(kc == 7), skip_group_check=True),
                                 reads=(f"wu{ws}", "x1full"), writes=(PB[ubank],))
                        gs = (hf * 2 + half) % 2
                        S.op("S", "activation",
                             (gsil_sb[gs][:, 0:512], pb[gbank][:, 0:512], AF.Silu),
                             dict(scale=INV127),
                             reads=(PB[gbank],), writes=(f"gsil{gs}",))
                        S.op("V", "tensor_scalar",
                             (ub_sb[gs][:, 0:512], pb[ubank][:, 0:512], INV127SQ, None, ALU.mult),
                             reads=(PB[ubank],), writes=(f"ub{gs}",))
                        ahc = 4 * h + hf
                        S.op("V", "tensor_tensor",
                             (a_sb[:, 1024 * ahc + 512 * half:1024 * ahc + 512 * (half + 1)],
                              gsil_sb[gs][:, 0:512], ub_sb[gs][:, 0:512], ALU.mult),
                             reads=(f"gsil{gs}", f"ub{gs}"), writes=("a",))

        def s4_down(r):
            if debug:
                S.op("Y", "dma_start", kw=dict(out=dbg["a4"].ap()[r], in_=a_sb[:, :]),
                     reads=("a",), grp="dbg")
                if r == 0:
                    S.op("Y", "dma_start", kw=dict(out=dbg["x1full"].ap(), in_=x1full_sb[:, :]),
                         reads=("x1full",), grp="dbg")
            for fcg in range(2):
                for hcp in range(12):
                    ws = hcp % 3
                    S.op("G", "dma_start",
                         kw=dict(out=wd_sb[ws][:, :].rearrange("p (c e) -> p c e", e=1024),
                                 in_=d["wdT"].ap()[256 * hcp:256 * (hcp + 1), :].rearrange(
                                     "(c p) e -> p c e", p=128)),
                         writes=(f"wd{ws}",), grp=f"wd{ws}")
                    for hi in range(2):
                        hc = 2 * hcp + hi
                        for fi in range(4):
                            fc = 4 * fcg + fi
                            for half in range(2):
                                bank = fi * 2 + half
                                S.op("T", "matmul",
                                     (pb[bank][:, 0:512],
                                      wd_sb[ws][:, 1024 * hi + 128 * fc:1024 * hi + 128 * (fc + 1)],
                                      a_sb[:, 1024 * hc + 512 * half:1024 * hc + 512 * (half + 1)]),
                                     dict(start=(hc == 0), stop=(hc == 23), skip_group_check=True),
                                     reads=(f"wd{ws}", "a"), writes=(PB[bank],))
                for fi in range(4):
                    fc = 4 * fcg + fi
                    for half in range(2):
                        bank = fi * 2 + half
                        cc = 1024 * fc + 512 * half
                        x1c = 2048 * fc + 1024 * r + 512 * half
                        S.op("V", "tensor_tensor",
                             (z2_sb[:, cc:cc + 512], pb[bank][:, 0:512],
                              x1full_sb[:, x1c:x1c + 512], ALU.add),
                             reads=(PB[bank], "x1full"), writes=("z2",))

        def s4_ln2(r):
            if debug:
                S.op("Y", "dma_start", kw=dict(out=dbg["z2"].ap()[r], in_=z2_sb[:, :]),
                     reads=("z2",), grp="dbg")
            for half in range(2):
                cols = [1024 * c + 512 * half for c in range(8)]
                for c in range(8):
                    S.op("S", "activation",
                         (a_sb[:, cols[c]:cols[c] + 512],
                          z2_sb[:, cols[c]:cols[c] + 512], AF.Copy),
                         reads=("z2",), writes=("a",))
                _ln(S, pb, PB, c_ones, c_onesrow, stt_sb, "stt3",
                    z2_sb, "z2", (a_sb, cols), "a",
                    zsqb_sb, ("zsqb0", "zsqb1"),
                    outh_sb, "outh", 512, cols, (4, 5, 6, 7),
                    dst_cols=[512 * c for c in range(8)])
                S.op("Y", "dma_start",
                     kw=dict(out=y.ap()[:, 1024 * r + 512 * half:1024 * r + 512 * (half + 1)]
                             .rearrange("(c p) n -> p c n", p=128),
                             in_=outh_sb[:, :].rearrange("p (c n) -> p c n", n=512)),
                     reads=("outh",), grp="yst")

        s4_gateup(0)
        s4_down(0)
        s4_ln2(0)
        s4_gateup(1)
        s4_down(1)
        s4_ln2(1)

        s4.close()
        finish()
    return nc



def _window_mask_blocks():
    std = np.zeros((128, 122), dtype=np.float32)
    for t in range(122):
        std[t:t + 7, t] = 1.0
    w0 = std.copy()
    w0[0:3, :] = 0.0
    w0[0, 0], w0[0, 1], w0[0, 2] = 3.0, 2.0, 1.0
    wl = np.zeros((128, 96), dtype=np.float32)
    for t in range(96):
        wl[t:t + 7, t] = 1.0
    wl[102:, :] = 0.0
    wl7 = wl.copy()
    wl7[99:102, :] = 0.0
    wl7[99, 93], wl7[99, 94], wl7[99, 95] = 1.0, 2.0, 3.0
    return std, w0, wl, wl7


_CACHE = {}


def kernel(L0_attn_x, L0_attn_W_qkv, L0_attn_b_qkv, L0_attn_W_lat, L0_attn_b_lat,
           L0_attn_W_proj, L0_attn_b_proj, L0_ffn_w_gate, L0_ffn_w_up, L0_ffn_w_down):
    bf = ml_dtypes.bfloat16
    x = np.ascontiguousarray(np.asarray(L0_attn_x, dtype=np.float32)[0])
    Wqkv = np.asarray(L0_attn_W_qkv, dtype=np.float32)
    bqkv = np.asarray(L0_attn_b_qkv, dtype=np.float32)
    Wlat = np.asarray(L0_attn_W_lat, dtype=np.float32)
    blat = np.asarray(L0_attn_b_lat, dtype=np.float32)
    Wproj = np.asarray(L0_attn_W_proj, dtype=np.float32)
    bproj = np.asarray(L0_attn_b_proj, dtype=np.float32)

    wq_lay = np.ascontiguousarray(Wqkv.T)
    wqkv_r = np.ascontiguousarray(wq_lay.reshape(8, 128, 24, 128).transpose(2, 0, 1, 3))
    wprojT = np.ascontiguousarray(Wproj.T)
    wlat2 = np.ascontiguousarray(np.vstack([Wlat.T, Wlat.T]))
    wgT = np.ascontiguousarray(np.asarray(L0_ffn_w_gate, dtype=np.int32).T)
    wuT = np.ascontiguousarray(np.asarray(L0_ffn_w_up, dtype=np.int32).T)
    wdT = np.ascontiguousarray(np.asarray(L0_ffn_w_down, dtype=np.int32).T)
    bqkv_col = np.ascontiguousarray(bqkv.reshape(24, 128).T)
    blat_col = np.ascontiguousarray(np.concatenate([blat, blat])[:, None])
    blat_b512 = np.ascontiguousarray(np.tile(blat, (128, 8)))
    bproj_col = np.ascontiguousarray(bproj.reshape(8, 128).T)
    ones_col = np.ones((128, 1), dtype=bf)
    ones_row_f32 = np.ones((128, 128), dtype=np.float32)
    ones_sel = np.zeros((128, 256), dtype=np.float32)
    for h in range(16):
        ones_sel[:, 16 * h + h] = 1.0
    ind = np.zeros((128, 128), dtype=np.float32)
    for c in range(8):
        ind[0:64, 16 * c + 2 * c] = 1.0
        ind[64:128, 16 * c + 2 * c + 1] = 1.0
    selbc = np.zeros((128, 16 * 128), dtype=np.float32)
    for h in range(16):
        selbc[h, 128 * h:128 * (h + 1)] = 1.0

    std, w0blk, wl_std, wl7 = _window_mask_blocks()
    mqs = np.zeros((128, 488), dtype=np.float32)
    for j in range(4):
        mqs[:, 122 * j:122 * (j + 1)] = std

    key = ("prog", STAGE, DEBUG)
    if key not in _CACHE:
        _CACHE[key] = build_program(STAGE, DEBUG)
    nc = _CACHE[key]

    in_maps = []
    for c in range(NCORE):
        base = 2048 * c
        rows = np.arange(base - 3, base + 2051)
        if c == 0:
            rows = rows.copy()
            rows[0] = SEQ - 1
            rows[1] = -1
            rows[2] = -1
        if c == NCORE - 1:
            rows = rows.copy()
            rows[2051] = 0
            rows[2052] = -1
            rows[2053] = -1
        xc = np.zeros((A, E), dtype=np.float32)
        valid = rows >= 0
        xc[valid] = x[rows[valid]]
        xT = np.ascontiguousarray(xc.T)

        mq0 = mqs.copy()
        mq0[:, 0:122] = w0blk if c == 0 else std
        mlast = wl7 if c == NCORE - 1 else wl_std

        in_maps.append(dict(
            xT=xT, wqkv_r=wqkv_r, wprojT=wprojT, wlat2=wlat2,
            wgT=wgT, wuT=wuT, wdT=wdT,
            bqkv_col=bqkv_col, blat_col=blat_col, blat_b512=blat_b512,
            bproj_col=bproj_col,
            mq0=mq0.astype(bf), mqs=mqs.astype(bf), mlast=mlast.astype(bf),
            ones_col=ones_col, ones_row_f32=ones_row_f32,
            ones_sel=ones_sel.astype(bf), ones128=np.full((128, 128), 2.0**29, dtype=bf),
            b29=np.full((128, 1), 29.0 * np.log(2.0), dtype=np.float32),
            ones_col_f32=np.ones((128, 1), dtype=np.float32),
            selbc=selbc.astype(bf), ind=ind.astype(bf),
        ))

    res = run_bass_kernel_spmd(nc, in_maps, core_ids=list(range(NCORE)))
    out = np.empty((1, SEQ, E), dtype=np.float32)
    for c in range(NCORE):
        out[0, 2048 * c:2048 * (c + 1), :] = np.asarray(res.results[c]["y"]).T
    if DEBUG:
        kernel._last = res
    return out
